# version 06
# baseline (speedup 1.0000x reference)
import numpy as np
import ml_dtypes
import concourse.bass as bass
import concourse.mybir as mybir
from concourse.bass_utils import run_bass_kernel_spmd

F32 = mybir.dt.float32
BF16 = mybir.dt.bfloat16
AX = mybir.AxisListType.X
ALU = mybir.AluOpType
AF = mybir.ActivationFunctionType

COMPUTE = ("PE", "DVE", "ACT", "POOL", "SP")
QUEUES = {"SQ": "SP", "GQ": "POOL", "AQ": "ACT"}
NQS = 8


class _Node:
    __slots__ = ("eng", "seq", "fn", "deps", "need_inc", "tag", "queue", "qseq", "lid")

    def __init__(self, eng, seq, fn, tag):
        self.eng = eng
        self.seq = seq
        self.fn = fn
        self.deps = {}
        self.need_inc = False
        self.tag = tag
        self.queue = None
        self.qseq = None


class Prog:

    def __init__(self, nc):
        self.nc = nc
        self.streams = {e: [] for e in COMPUTE}
        self.qcount = {q: 0 for q in QUEUES}
        self.bufs = {}
        self._prev_pe_rows = None
        self.cur_lid = None
        self._lidn = 0
        self.lstate = {}
        self.lbad = []

    def _audit(self, lid, reads, writes, tag):
        if lid is None:
            return
        for k in reads:
            if k.startswith("pB") or k.startswith("pb") or k.startswith("spb"):
                continue
            st = self.lstate.get(k)
            if st is None:
                self.lstate[k] = [None, lid, None]
                continue
            if st[0] is not None and st[0] > lid:
                self.lbad.append((k, "stale-read", st[0][:2], lid[:2], tag, st[2]))
            if st[1] is None or lid > st[1]:
                st[1] = lid
        for k in writes:
            if k.startswith("pB") or k.startswith("pb") or k.startswith("spb"):
                continue
            st = self.lstate.get(k)
            if st is None:
                self.lstate[k] = [lid, lid, tag]
                continue
            if st[1] is not None and st[1] > lid:
                self.lbad.append((k, "late-write", st[1][:2], lid[:2], tag, st[2]))
            if st[0] is None or lid > st[0]:
                st[0] = lid
            if st[1] is None or lid > st[1]:
                st[1] = lid
            st[2] = tag

    def _depend(self, node, ev):
        if ev is None:
            return
        dom, seq = ev
        if dom == node.eng:
            if seq >= node.seq:
                return
            if dom == "PE":
                return
        key = (dom, seq % NQS) if dom in QUEUES else dom
        if seq > node.deps.get(key, -1):
            node.deps[key] = seq

    def _touch(self, node_ev, node, reads, writes):
        for k in reads:
            ent = self.bufs.get(k)
            if ent is not None:
                self._depend(node, ent[0])
        for k in writes:
            ent = self.bufs.get(k)
            if ent is not None:
                self._depend(node, ent[0])
                for r in ent[1]:
                    self._depend(node, r)
        for k in reads:
            self.bufs.setdefault(k, [None, []])[1].append(node_ev)
        for k in writes:
            self.bufs[k] = [node_ev, []]

    def op(self, eng, fn, reads=(), writes=(), tag="", pe_rows=(0, 128)):
        seq = len(self.streams[eng])
        node = _Node(eng, seq, fn, tag)
        node.lid = None if self.cur_lid is None else (self.cur_lid[0], self.cur_lid[1], self._lidn)
        self._lidn += 1
        self.streams[eng].append(node)
        if eng == "PE":
            pr = self._prev_pe_rows
            if pr is not None and seq > 0 and (pe_rows[1] <= pr[0] or pr[1] <= pe_rows[0]):
                if node.deps.get("PE", -1) < seq - 1:
                    node.deps["PE"] = seq - 1
            self._prev_pe_rows = pe_rows
        self._touch((eng, seq), node, reads, writes)
        self._audit(node.lid, reads, writes, tag)
        return node

    def dma(self, queue, fn, reads=(), writes=(), tag=""):
        issuer = QUEUES[queue]
        seq = len(self.streams[issuer])
        node = _Node(issuer, seq, fn, tag)
        node.lid = None if self.cur_lid is None else (self.cur_lid[0], self.cur_lid[1], self._lidn)
        self._lidn += 1
        node.queue = queue
        node.qseq = self.qcount[queue]
        self.qcount[queue] += 1
        self.streams[issuer].append(node)
        self._touch((queue, node.qseq), node, reads, writes)
        self._audit(node.lid, reads, writes, tag)
        return node

    def global_barrier(self):
        last_compute = {}
        for e in COMPUTE:
            for i in range(len(self.streams[e]) - 1, -1, -1):
                if self.streams[e][i].queue is None:
                    last_compute[e] = i
                    break
        qs = dict(self.qcount)
        for e in COMPUTE:
            node = self.op(e, lambda en: en.nop(), tag="barrier")
            for e2, i2 in last_compute.items():
                if e2 != e:
                    self._depend(node, (e2, i2))
            for q, n in qs.items():
                if n > 0:
                    self._depend(node, (q, n - 1))

    def emit(self):
        if self.lbad:
            raise AssertionError("logical-order hazards: " + repr(self.lbad[:20]))
        nc = self.nc
        for e in COMPUTE:
            for node in self.streams[e]:
                for dom, seq in node.deps.items():
                    if not isinstance(dom, tuple):
                        self.streams[dom][seq].need_inc = True
        prefix = {}
        for e in COMPUTE:
            c, pl = 0, []
            for node in self.streams[e]:
                if node.need_inc:
                    c += 1
                pl.append(c)
            prefix[e] = pl
        self._prefix = prefix

        ctx, semh = [], {}
        for nm in COMPUTE:
            cm = nc.semaphore("sem_" + nm)
            ctx.append(cm)
            semh[nm] = cm.__enter__()
        for qn in QUEUES:
            for i in range(NQS):
                cm = nc.semaphore(f"sem_{qn}{i}")
                ctx.append(cm)
                semh[(qn, i)] = cm.__enter__()

        blk_cm = nc.Block()
        blk = blk_cm.__enter__()
        for e in COMPUTE:
            section = getattr(blk, {"PE": "tensor", "DVE": "vector", "ACT": "scalar",
                                    "POOL": "gpsimd", "SP": "sync"}[e])

            def body(eng_obj, e=e):
                waited = {}
                for node in self.streams[e]:
                    for key, seq in sorted(node.deps.items(), key=str):
                        if isinstance(key, tuple):
                            val = 16 * (seq // NQS + 1)
                        else:
                            val = prefix[key][seq]
                            if val == 0:
                                continue
                        if waited.get(key, -1) >= val:
                            continue
                        eng_obj.wait_ge(semh[key], val)
                        waited[key] = val
                    if node.queue is not None and node.qseq >= NQS:
                        key = (node.queue, node.qseq % NQS)
                        val = 16 * (node.qseq // NQS)
                        if waited.get(key, -1) < val:
                            eng_obj.wait_ge(semh[key], val)
                            waited[key] = val
                    inst = node.fn(eng_obj)
                    if node.queue is not None:
                        inst.then_inc(semh[(node.queue, node.qseq % NQS)], 16)
                    elif node.need_inc:
                        inst.then_inc(semh[e], 1)
                if e == "SP":
                    for q in QUEUES:
                        for i in range(NQS):
                            cnt = (self.qcount[q] - i + NQS - 1) // NQS
                            if cnt > 0:
                                eng_obj.wait_ge(semh[(q, i)], 16 * cnt)

            section(body)
        blk_cm.__exit__(None, None, None)
        for cm in ctx:
            cm.__exit__(None, None, None)

    def stats(self):
        s = {e: len(self.streams[e]) for e in COMPUTE}
        s.update({q: self.qcount[q] for q in QUEUES})
        s["incs"] = {e: (self._prefix[e][-1] if self.streams[e] else 0) for e in COMPUTE}
        return s


D = 1024
NH, DK, DV = 12, 64, 128
QD, VD, FF = 768, 1536, 4096
CH = 128
TA = 1024
TB = 512


DBG_G = 2


def build_program(T, dbg=False, phase="full", rec_limit=10**9, rec_steps=10**9):
    NTA = T // TA
    NTB = T // TB
    NCH = T // CH
    nc = bass.Bass("TRN2", target_bir_lowering=False)

    def din(name, shape, dt=F32):
        return nc.dram_tensor(name, list(shape), dt, kind="ExternalInput").ap()

    x_d = din("x", (T, D))
    wq_d = din("wq", (D, QD))
    wk_d = din("wk", (D, QD))
    wv_d = din("wv", (D, VD))
    wg_d = din("wg", (D, VD))
    wgkb_d = din("wgkb", (D, 24))
    w1t_d = din("w1t", (D, FF))
    w2t_d = din("w2t", (FF, D))
    wo_d = din("wo", (VD, D))
    b1_d = din("b1", (128, 32))
    b2_d = din("b2", (128, 8))
    cq_d = din("cq", (128, 6, 4))
    ck_d = din("ck", (128, 6, 4))
    cv_d = din("cv", (128, 12, 4))
    alog_d = din("alog", (1, NH))
    dtbb_d = din("dtbb", (1, 24))
    gnorm_d = din("gnorm", (1, DV))
    identf_d = din("identf", (128, 128))
    identb_d = din("identb", (128, 128), BF16)
    e64_d = din("e64", (128, 128), BF16)
    utones_d = din("utones", (128, 128))
    masks4_d = din("masks4", (128, 512), BF16)
    i4_d = din("i4", (128, 512), BF16)
    onesrow_d = din("onesrow", (1, 128))
    negonesrow_d = din("negonesrow", (1, 128))

    y_d = nc.dram_tensor("y", [T, D], F32, kind="ExternalOutput").ap()

    w1tern_d = nc.dram_tensor("w1tern", [D, FF], BF16, kind="Internal").ap()
    w2tern_d = nc.dram_tensor("w2tern", [FF, D], BF16, kind="Internal").ap()
    xnfm_d = nc.dram_tensor("xnfm", [NTB, 128, 8, TB], BF16, kind="Internal").ap()
    sg_d = nc.dram_tensor("sgd", [NTB, 128, 8, TB], BF16, kind="Internal").ap()
    crows_d = nc.dram_tensor("crowsd", [NCH, NH, 128], F32, kind="Internal").ap()
    cfm_d = nc.dram_tensor("cfmd", [NCH, NH, 128], F32, kind="Internal").ap()

    p = Prog(nc)

    pctx = []

    def SBP(name, shape, dt=F32):
        cm = nc.sbuf_tensor(name, list(shape), dt)
        pctx.append(cm)
        return cm.__enter__()

    identf = SBP("identf_sb", (128, 128))
    identb = SBP("identb_sb", (128, 128), BF16)
    utones = SBP("utones_sb", (128, 128))
    onesrow = SBP("onesrow_sb", (1, 128))
    negonesrow = SBP("negonesrow_sb", (1, 128))
    c_res = SBP("c_res", (128, NCH, NH))
    b_res = SBP("b_res", (128, NCH, NH))
    be_res = SBP("be_res", (128, NCH, NH))
    epsln = SBP("epsln", (128, 1))
    eps12 = SBP("eps12", (128, 1))
    onecol = SBP("onecol", (128, 1))

    p.dma("SQ", lambda e: e.dma_start(identf[:], identf_d), writes=["identf"])
    p.dma("SQ", lambda e: e.dma_start(identb[:], identb_d), writes=["identb"])
    p.dma("SQ", lambda e: e.dma_start(utones[:], utones_d), writes=["utones"])
    p.dma("SQ", lambda e: e.dma_start(onesrow[:], onesrow_d), writes=["onesrow"])
    p.dma("SQ", lambda e: e.dma_start(negonesrow[:], negonesrow_d), writes=["negonesrow"])
    p.op("POOL", lambda e: e.memset(epsln[:], 1e-5), writes=["epsln"])
    p.op("POOL", lambda e: e.memset(eps12[:], 1e-12), writes=["eps12"])
    p.op("POOL", lambda e: e.memset(onecol[:], 1.0), writes=["onecol"])

    actx = []

    def SBA(name, shape, dt=F32):
        cm = nc.sbuf_tensor(name, list(shape), dt)
        actx.append(cm)
        return cm.__enter__()

    xs = [SBA(f"xs{i}", (128, D)) for i in range(2)]
    sqb = [SBA(f"sqb{i}", (128, D)) for i in range(2)]
    xnt = [SBA(f"xnt{i}", (128, D)) for i in range(2)]
    stat = SBA("stat", (128, 8))
    xnhi = SBA("xnhi", (128, 8, TA), BF16)
    xnf32 = [SBA(f"xnf32{i}", (128, 8, 128)) for i in range(2)]
    wgkb_s = SBA("wgkb_s", (128, 8, 24))
    dtbb_s = SBA("dtbb_s", (128, 24))
    negea_s = SBA("negea_s", (128, NH))
    zb = [SBA(f"zb{i}", (128, 24)) for i in range(2)]
    spb = SBA("spb", (128, 36))
    cfm_s = SBA("cfm_s", (NH, 128))
    w1p = [SBA(f"w1p{i}", (128, 8, 512), BF16) for i in range(2)]
    w2full = SBA("w2full", (128, 32, D), BF16)
    hbuf = SBA("hbuf", (128, 32, 512), BF16)
    sgo = SBA("sgo", (128, 8, 512), BF16)
    b1s = SBA("b1s_sb", (128, 32))
    b2s = SBA("b2s_sb", (128, 8))
    wtf = [SBA(f"wtf{i}", (128, 1024)) for i in range(2)]
    asum = SBA("asum", (128, 40))
    thrc = SBA("thrc", (128, 2))
    terno = SBA("terno", (128, 1024), BF16)
    ternsc = SBA("ternsc", (128, 1024))

    psA, pfA = [], []
    for i in range(8):
        cm = nc.psum_tensor(f"pA{i}", [128, 512], F32)
        psA.append(cm)
        pfA.append(cm.__enter__())

    def ternarize(src_d, dst_d, tagp):
        nrows, ncols = src_d.shape
        colw = min(1024, ncols)
        pieces = [(r, cc) for r in range(nrows // 128) for cc in range(ncols // colw)]
        for i, (r, cc) in enumerate(pieces):
            buf = wtf[i % 2]
            kb = f"{tagp}wtf{i % 2}"
            p.dma("SQ", lambda e, r=r, cc=cc, buf=buf: e.dma_start(
                buf[:, 0:colw], src_d[r * 128:(r + 1) * 128, cc * colw:(cc + 1) * colw]),
                writes=[kb], tag="tern_ld")
            p.op("DVE", lambda e, i=i, buf=buf: e.tensor_reduce(
                asum[:, i % 32:i % 32 + 1], buf[:, 0:colw], axis=AX, op=ALU.add,
                apply_absolute_value=True) if i < 32 else e.scalar_tensor_tensor(
                asum[:, i % 32:i % 32 + 1], buf[:, 0:colw], 0.0, asum[:, i % 32:i % 32 + 1],
                ALU.bypass, ALU.add),
                reads=[kb, "asum"], writes=["asum"], tag="tern_red")
        npieces = len(pieces)
        assert npieces <= 32
        p.op("DVE", lambda e: e.reduce_sum(asum[:, 32:33], asum[:, 0:npieces], axis=AX),
             reads=["asum"], writes=["asum32"])
        p.op("PE", lambda e: e.matmul(pfA[7][0:1, 0:1], asum[:, 32:33], onecol[:],
                                      start=True, stop=True),
             reads=["asum32", "onecol"], writes=["pA7"], tag="tern_tot")
        nelem = float(nrows * ncols)
        p.op("DVE", lambda e: e.tensor_scalar(asum[0:1, 34:35], pfA[7][0:1, 0:1],
                                              1.0 / nelem, 1e-8, ALU.mult, ALU.max),
             reads=["pA7"], writes=["thr0"])
        p.op("DVE", lambda e: e.tensor_scalar(asum[0:1, 35:36], asum[0:1, 34:35],
                                              0.5, None, ALU.mult),
             reads=["thr0"], writes=["thr1"])
        p.op("PE", lambda e: e.matmul(pfA[7][:, 0:1], onesrow[:], asum[0:1, 35:36],
                                      start=True, stop=True),
             reads=["thr1", "onesrow"], writes=["pA7"], tag="tern_bc")
        p.op("DVE", lambda e: e.tensor_copy(thrc[:, 0:1], pfA[7][:, 0:1]),
             reads=["pA7"], writes=["thrc"])
        p.op("DVE", lambda e: e.tensor_scalar(thrc[:, 1:2], thrc[:, 0:1], -1.0, None, ALU.mult),
             reads=["thrc"], writes=["thrc2"])
        for i, (r, cc) in enumerate(pieces):
            buf = wtf[i % 2]
            kb = f"{tagp}wtf{i % 2}"
            p.dma("SQ", lambda e, r=r, cc=cc, buf=buf: e.dma_start(
                buf[:, 0:colw], src_d[r * 128:(r + 1) * 128, cc * colw:(cc + 1) * colw]),
                writes=[kb], tag="tern_ld2")
            p.op("DVE", lambda e, buf=buf: e.tensor_single_scalar(
                ternsc[:, 0:colw], buf[:, 0:colw], thrc[:, 1:2], ALU.is_lt),
                reads=[kb, "thrc2"], writes=["ternsc"], tag="tern_lt")
            p.op("DVE", lambda e, buf=buf: e.scalar_tensor_tensor(
                terno[:, 0:colw], buf[:, 0:colw], thrc[:, 0:1], ternsc[:, 0:colw],
                ALU.is_gt, ALU.subtract),
                reads=[kb, "thrc", "ternsc"], writes=["terno"], tag="tern_gt")
            p.dma("SQ", lambda e, r=r, cc=cc: e.dma_start(
                dst_d[r * 128:(r + 1) * 128, cc * colw:(cc + 1) * colw], terno[:, 0:colw]),
                reads=["terno"], writes=[f"{tagp}tern_done"], tag="tern_st")

    ternarize(w1t_d, w1tern_d, "a")
    ternarize(w2t_d, w2tern_d, "b")

    p.dma("GQ", lambda e: e.dma_start(wgkb_s[:], wgkb_d.rearrange("(c p) q -> p c q", p=128)),
          writes=["wgkb_s"])
    p.dma("SQ", lambda e: e.dma_start(b1s[:], b1_d), writes=["b1s"])
    p.dma("SQ", lambda e: e.dma_start(b2s[:], b2_d), writes=["b2s"])
    p.dma("SQ", lambda e: e.dma_start(spb[0:1, 0:24], dtbb_d), writes=["spb0", "spb1"])
    p.dma("SQ", lambda e: e.dma_start(spb[0:1, 24:36], alog_d), writes=["spb2"])
    p.op("ACT", lambda e: e.activation(spb[0:1, 24:36], spb[0:1, 24:36], AF.Exp),
         reads=["spb2"], writes=["spb2"], tag="ExpA")
    p.op("DVE", lambda e: e.tensor_scalar(spb[0:1, 24:36], spb[0:1, 24:36], -1.0, None, ALU.mult),
         reads=["spb2"], writes=["spb2"])
    p.op("PE", lambda e: e.matmul(pfA[6][:, 0:24], onesrow[:], spb[0:1, 0:24],
                                  start=True, stop=True),
         reads=["spb0", "spb1", "onesrow"], writes=["pA6"])
    p.op("DVE", lambda e: e.tensor_copy(dtbb_s[:], pfA[6][:, 0:24]),
         reads=["pA6"], writes=["dtbb_s"])
    p.op("PE", lambda e: e.matmul(pfA[6][:, 32:44], onesrow[:], spb[0:1, 24:36],
                                  start=True, stop=True),
         reads=["spb2", "onesrow"], writes=["pA6"])
    p.op("DVE", lambda e: e.tensor_copy(negea_s[:], pfA[6][:, 32:44]),
         reads=["pA6"], writes=["negea_s"])

    def emit_ln_strip(a, s):
        gs = a * 8 + s
        xb, sq, xn, xf = xs[gs % 2], sqb[gs % 2], xnt[gs % 2], xnf32[gs % 2]
        kx, ksq, kxn, kxf = (f"xs{gs % 2}", f"sq{gs % 2}", f"xnt{gs % 2}", f"xnf{gs % 2}")
        p.dma("SQ", lambda e: e.dma_start(xb[:], x_d[gs * 128:(gs + 1) * 128, :]),
              writes=[kx], tag="ldx")
        p.op("DVE", lambda e: e.tensor_mul(sq[:], xb[:], xb[:]), reads=[kx], writes=[ksq])
        p.op("DVE", lambda e: e.reduce_sum(stat[:, 0:1], xb[:], axis=AX),
             reads=[kx], writes=["stat0"])
        p.op("DVE", lambda e: e.reduce_sum(stat[:, 1:2], sq[:], axis=AX),
             reads=[ksq], writes=["stat1"])
        p.op("DVE", lambda e: e.tensor_scalar(stat[:, 2:3], stat[:, 0:1], 1.0 / D, None, ALU.mult),
             reads=["stat0"], writes=["stat2"])
        p.op("DVE", lambda e: e.tensor_scalar(stat[:, 3:4], stat[:, 1:2], 1.0 / D, None, ALU.mult),
             reads=["stat1"], writes=["stat3"])
        p.op("DVE", lambda e: e.scalar_tensor_tensor(stat[:, 4:5], stat[:, 2:3], stat[:, 2:3],
                                                     stat[:, 3:4], ALU.mult, ALU.subtract),
             reads=["stat2", "stat3"], writes=["stat4"])
        p.op("DVE", lambda e: e.tensor_scalar(stat[:, 5:6], stat[:, 4:5], -1.0, None, ALU.mult),
             reads=["stat4"], writes=["stat5"])
        p.op("ACT", lambda e: e.activation(stat[:, 6:7], stat[:, 5:6], AF.Ln, bias=epsln[:]),
             reads=["stat5", "epsln"], writes=["stat6"], tag="LnA")
        p.op("ACT", lambda e: e.activation(stat[:, 7:8], stat[:, 6:7], AF.Exp, scale=-0.5),
             reads=["stat6"], writes=["stat7"], tag="ExpA")
        p.op("DVE", lambda e: e.tensor_scalar(xn[:], xb[:], stat[:, 2:3], stat[:, 7:8],
                                              ALU.subtract, ALU.mult),
             reads=[kx, "stat2", "stat7"], writes=[kxn], tag="ln")
        for c in range(8):
            bank = c % 4
            p.op("PE", lambda e, c=c, bank=bank: e.transpose(
                pfA[bank][:, 0:128], xn[:, c * 128:(c + 1) * 128], identf[:]),
                reads=[kxn, "identf"], writes=[f"pA{bank}"], tag="tpA")
            p.op("DVE", lambda e, c=c, bank=bank: e.tensor_copy(
                xnhi[:, c, s * 128:(s + 1) * 128], pfA[bank][:, 0:128]),
                reads=[f"pA{bank}"], writes=["xnhi"], tag="hiev")
            p.op("DVE", lambda e, c=c, bank=bank: e.tensor_copy(
                xf[:, c, :], pfA[bank][:, 0:128]),
                reads=[f"pA{bank}"], writes=[kxf], tag="f32ev")
        for c in range(8):
            p.op("PE", lambda e, c=c: e.matmul(pfA[2][:, 0:24], xf[:, c, :], wgkb_s[:, c, :],
                                               start=(c == 0), stop=(c == 7)),
                 reads=[kxf, "wgkb_s"], writes=["pA2"], tag="gkmm")
        z = zb[gs % 2]
        kz = f"zb{gs % 2}"
        p.op("DVE", lambda e: e.tensor_add(z[:], pfA[2][:, 0:24], dtbb_s[:]),
             reads=["pA2", "dtbb_s"], writes=[kz])
        p.op("DVE", lambda e: e.tensor_scalar(spb[:, 0:12], z[:, 0:12], -1.0, None, ALU.mult),
             reads=[kz], writes=["spb0"])
        p.op("DVE", lambda e: e.tensor_max(spb[:, 0:12], spb[:, 0:12], z[:, 0:12]),
             reads=["spb0", kz], writes=["spb0"])
        p.op("ACT", lambda e: e.activation(spb[:, 12:24], spb[:, 0:12], AF.Exp, scale=-1.0),
             reads=["spb0"], writes=["spb1"], tag="ExpA")
        p.op("ACT", lambda e: e.activation(spb[:, 0:12], spb[:, 12:24], AF.Ln, bias=onecol[:]),
             reads=["spb1", "onecol"], writes=["spb0"], tag="LnA")
        p.op("DVE", lambda e: e.tensor_scalar(spb[:, 12:24], z[:, 0:12], 0.0, None, ALU.max),
             reads=[kz], writes=["spb1"])
        p.op("DVE", lambda e: e.tensor_add(spb[:, 24:36], spb[:, 0:12], spb[:, 12:24]),
             reads=["spb0", "spb1"], writes=["spb2"])
        p.op("DVE", lambda e: e.tensor_mul(spb[:, 0:12], spb[:, 24:36], negea_s[:]),
             reads=["spb2", "negea_s"], writes=["spb0"])
        p.op("PE", lambda e: e.matmul(pfA[3][:, 0:12], utones[:], spb[:, 0:12],
                                      start=True, stop=True),
             reads=["spb0", "utones"], writes=["pA3"], tag="cums")
        p.op("DVE", lambda e, gs=gs: e.tensor_copy(c_res[:, gs, :], pfA[3][:, 0:12]),
             reads=["pA3"], writes=["c_res"])
        p.op("ACT", lambda e, gs=gs: e.activation(b_res[:, gs, :], pfA[3][:, 0:12], AF.Exp),
             reads=["pA3"], writes=["b_res"], tag="ExpA")
        p.op("ACT", lambda e: e.activation(spb[:, 12:24], z[:, 12:24], AF.Exp, scale=-1.0),
             reads=[kz], writes=["spb1"], tag="ExpA")
        p.op("DVE", lambda e: e.tensor_scalar(spb[:, 24:36], spb[:, 12:24], 1.0, None, ALU.add),
             reads=["spb1"], writes=["spb2"])
        p.op("DVE", lambda e, gs=gs: e.reciprocal(be_res[:, gs, :], spb[:, 24:36]),
             reads=["spb2"], writes=["be_res"])
        p.op("PE", lambda e, gs=gs: e.transpose(pfA[4][0:12, 0:128], c_res[:, gs, :], identf[:]),
             reads=["c_res", "identf"], writes=["pA4"], tag="ctp")
        p.op("DVE", lambda e: e.tensor_copy(cfm_s[:], pfA[4][0:12, 0:128]),
             reads=["pA4"], writes=["cfm_s"])
        p.dma("SQ", lambda e, gs=gs: e.dma_start(cfm_d[gs], cfm_s[:]),
              reads=["cfm_s"], writes=["cfm_d"], tag="cst")
        p.dma("SQ", lambda e, gs=gs: e.dma_start(
            crows_d[gs].rearrange("h t -> (h t)").unsqueeze(0),
            cfm_d[gs].rearrange("h t -> (h t)").unsqueeze(0)),
            reads=["cfm_d"], writes=["crows_done"], tag="cro")

    def emit_ffn(a):
        p.dma("SQ", lambda e: e.dma_start(
            w2full[:], w2tern_d.rearrange("(c p) q -> p c q", p=128)),
            reads=["btern_done"], writes=["w2full"], tag="w2ld")
        for sub in range(2):
            tk = slice(sub * 512, (sub + 1) * 512)
            for piece in range(8):
                buf = w1p[piece % 2]
                kb = f"w1p{piece % 2}"
                p.dma("SQ", lambda e, piece=piece, buf=buf: e.dma_start(
                    buf[:], w1tern_d.rearrange("(c p) f -> p c f", p=128)[:, :, piece * 512:(piece + 1) * 512]),
                    reads=["atern_done"], writes=[kb], tag="w1ld")
                for lf in range(4):
                    ffc = piece * 4 + lf
                    bank = ffc % 2
                    for c in range(8):
                        p.op("PE", lambda e, c=c, lf=lf, bank=bank, buf=buf, tk=tk: e.matmul(
                            pfA[bank][:, 0:512], buf[:, c, lf * 128:(lf + 1) * 128],
                            xnhi[:, c, tk], start=(c == 0), stop=(c == 7)),
                            reads=[kb, "xnhi"], writes=[f"pA{bank}"], tag="w1mm")
                    p.op("ACT", lambda e, ffc=ffc, bank=bank: e.activation(
                        hbuf[:, ffc, :], pfA[bank][:, 0:512], AF.Gelu, bias=b1s[:, ffc:ffc + 1]),
                        reads=[f"pA{bank}", "b1s"], writes=["hbuf"], tag="gelu")
            for e_ in range(8):
                for ffc in range(32):
                    p.op("PE", lambda e, e_=e_, ffc=ffc: e.matmul(
                        pfA[e_][:, 0:512], w2full[:, ffc, e_ * 128:(e_ + 1) * 128],
                        hbuf[:, ffc, :], start=(ffc == 0), stop=(ffc == 31)),
                        reads=["w2full", "hbuf"], writes=[f"pA{e_}"], tag="w2mm")
            for e_ in range(8):
                p.op("ACT", lambda e, e_=e_: e.activation(
                    sgo[:, e_, :], pfA[e_][:, 0:512], AF.Silu, bias=b2s[:, e_:e_ + 1]),
                    reads=[f"pA{e_}", "b2s"], writes=["sgo"], tag="ffsilu")
            tb_idx = a * 2 + sub
            p.dma("SQ", lambda e, tb_idx=tb_idx: e.dma_start(sg_d[tb_idx], sgo[:]),
                  reads=["sgo"], writes=["sg_done"], tag="sgst")

    for a in range(NTA):
        for s in range(8):
            emit_ln_strip(a, s)
        emit_ffn(a)
        for half in range(2):
            tb_idx = a * 2 + half
            p.dma("SQ", lambda e, tb_idx=tb_idx, half=half: e.dma_start(
                xnfm_d[tb_idx], xnhi[:, :, half * 512:(half + 1) * 512]),
                reads=["xnhi"], writes=["xn_done"], tag="xnst")

    p.global_barrier()
    for cm in reversed(psA):
        cm.__exit__(None, None, None)
    for cm in reversed(actx):
        cm.__exit__(None, None, None)

    bctx = []

    def SBB(name, shape, dt=F32):
        cm = nc.sbuf_tensor(name, list(shape), dt)
        bctx.append(cm)
        return cm.__enter__()

    wvg = [SBB(f"wvg{i}", (128, 8, 512), BF16) for i in range(2)]
    wo_s = [SBB(f"wo_s{i}", (128, 3, D), BF16) for i in range(2)]
    xnb = [SBB(f"xnb{i}", (128, 8, TB), BF16) for i in range(2)]
    sgb = SBB("sgb", (128, 8, TB), BF16)
    qs_ = SBB("qs", (128, 6, TB), BF16)
    ks_ = SBB("ks", (128, 6, TB), BF16)
    vs_ = SBB("vs", (128, 12, TB), BF16)
    gt_ = SBB("gt", (128, 4, VD), BF16)
    ext = [SBB(f"ext{i}", (128, 516), BF16) for i in range(3)]
    cvo = [SBB(f"cvo{i}", (128, 512), BF16) for i in range(2)]
    stash = SBB("stash", (128, 24, 4), BF16)
    sqv = SBB("sqv", (128, 512), BF16)
    rno = [SBB(f"rno{i}", (128, 512)) for i in range(2)]
    cqs = SBB("cqs_sb", (128, 6, 4))
    cks = SBB("cks_sb", (128, 6, 4))
    cvs = SBB("cvs_sb", (128, 12, 4))
    masks4 = SBB("masks4_sb", (128, 512), BF16)
    i4 = SBB("i4_sb", (128, 512), BF16)
    e64 = SBB("e64_sb", (128, 128), BF16)
    gnw4 = SBB("gnw4", (128, 512))
    crows = [SBB(f"crows{i}", (1, 1536)) for i in range(2)]
    bcrow = SBB("bcrow", (1, NH))
    bccol = SBB("bccol", (128, NH))
    S_ = SBB("S", (128, 6, DV))
    Sbf = SBB("Sbf", (128, 6, DV), BF16)
    e1lg = SBB("e1lg", (128, 512))
    e1s = [SBB(f"e1s{i}", (128, 512), BF16) for i in range(2)]
    e1i = [SBB(f"e1i{i}", (128, 512), BF16) for i in range(2)]
    sB0 = [SBB(f"sB0{i}", (128, 512), BF16) for i in range(2)]
    sA0 = [SBB(f"sA0{i}", (128, 512), BF16) for i in range(2)]
    sU = [SBB(f"sU{i}", (128, 512), BF16) for i in range(3)]
    sWd = [SBB(f"sWd{i}", (128, 512), BF16) for i in range(2)]
    skbb = [[SBB(f"skbb{i}_{hl}", (128, 256), BF16) for hl in range(2)] for i in range(2)]
    sWx = [[SBB(f"sWx{s_}_{hl}", (128, 256), BF16) for hl in range(2)] for s_ in range(2)]
    sWkF = [[SBB(f"sWkF{i}_{hl}", (128, 256), BF16) for hl in range(2)] for i in range(2)]
    wtf32 = [SBB(f"wtf32{i}", (128, 256)) for i in range(2)]
    sAx = [[SBB(f"sAx{s_}_{i}", (128, 512), BF16) for i in range(2)] for s_ in range(2)]
    sBx = [[SBB(f"sBx{s_}_{i}", (128, 512), BF16) for i in range(2)] for s_ in range(2)]
    sBt = [SBB(f"sBt{s_}", (128, 512), BF16) for s_ in range(2)]
    sDx = [[SBB(f"sDx{s_}_{hl}", (128, 512), BF16) for hl in range(2)] for s_ in range(2)]
    khat = [SBB(f"khat{i}", (128, 256), BF16) for i in range(2)]
    oS = [SBB(f"oS{i}", (128, 512)) for i in range(2)]
    o2S = [SBB(f"o2S{i}", (128, 512)) for i in range(2)]
    rsts = SBB("rsts", (128, 16))
    rcols = SBB("rcols", (128, 8))
    ogt = [SBB(f"ogt{i}", (128, 512), BF16) for i in range(2)]
    ofm = SBB("ofm", (128, 12, TB), BF16)
    xre = SBB("xre", (128, 4, D))

    psB, pfB = [], []
    for i in range(6):
        cm = nc.psum_tensor(f"pB{i}", [128, 512], F32)
        psB.append(cm)
        pfB.append(cm.__enter__())
    pb0cm = nc.psum_tensor("pb0", [128, 1024], BF16)
    psB.append(pb0cm)
    pb0 = pb0cm.__enter__()
    pb1cm = nc.psum_tensor("pb1", [128, 1024], BF16)
    psB.append(pb1cm)
    pb1 = pb1cm.__enter__()

    p.dma("SQ", lambda e: e.dma_start(masks4[:], masks4_d), writes=["masks4"])
    p.dma("SQ", lambda e: e.dma_start(i4[:], i4_d), writes=["i4"])
    p.dma("SQ", lambda e: e.dma_start(e64[:], e64_d), writes=["e64"])
    p.dma("SQ", lambda e: e.dma_start(cqs[:], cq_d), writes=["cqs"])
    p.dma("SQ", lambda e: e.dma_start(cks[:], ck_d), writes=["cks"])
    p.dma("SQ", lambda e: e.dma_start(cvs[:], cv_d), writes=["cvs"])
    p.dma("SQ", lambda e: e.dma_start(gnw4[0:1, 0:DV], gnorm_d), writes=["gnwrow"])
    for q in range(4):
        p.op("PE", lambda e, q=q: e.matmul(pfB[0][:, q * 128:(q + 1) * 128], onesrow[:],
                                           gnw4[0:1, 0:DV], start=(q == 0), stop=(q == 3)),
             reads=["gnwrow", "onesrow"], writes=["pB0"])
    p.op("DVE", lambda e: e.tensor_copy(gnw4[:], pfB[0][:, 0:512]),
         reads=["pB0"], writes=["gnw4"])
    p.op("POOL", lambda e: e.memset(S_[:], 0.0), writes=["S"])
    p.op("POOL", lambda e: e.memset(Sbf[:], 0.0), writes=["Sbf"])
    p.op("POOL", lambda e: e.memset(stash[:], 0.0), writes=["stash"])
    p.dma("SQ", lambda e: e.dma_start(
        crows[0][:], crows_d[0].rearrange("h t -> (h t)").unsqueeze(0)),
        reads=["crows_done"], writes=["crows0"], tag="crold0")

    def emit_frontend(t):
        sl = t % 2
        kxn = f"xnb{sl}"
        p.dma("SQ", lambda e, t=t, sl=sl: e.dma_start(xnb[sl][:], xnfm_d[t]),
              reads=["xn_done"], writes=[kxn], tag="xnld")
        pool_ctr = [0]
        for name, nch, cw, dst, kdst, wsrc, cpp in [
                ("q", 6, cqs, qs_, "qs", wq_d, 3), ("k", 6, cks, ks_, "ks", wk_d, 3),
                ("v", 12, cvs, vs_, "vs", wv_d, 4)]:
            for j in range(nch):
                bank = j % 2
                kbank = f"pB{bank}"
                if j % cpp == 0:
                    pi = pool_ctr[0]
                    pool_ctr[0] += 1
                    buf_ = wvg[pi % 2]
                    w0 = j * 128
                    wwid = cpp * 128
                    p.dma("GQ", lambda e, buf_=buf_, w0=w0, wwid=wwid, wsrc=wsrc: e.dma_start(
                        buf_[:, :, 0:wwid],
                        wsrc.rearrange("(c p) q -> p c q", p=128)[:, :, w0:w0 + wwid]),
                        writes=[f"wvg{pi % 2}"], tag="wpld")
                pi = pool_ctr[0] - 1
                buf = wvg[pi % 2]
                wkey = f"wvg{pi % 2}"
                wap = lambda c, j=j, buf=buf, cpp=cpp: buf[:, c, (j % cpp) * 128:(j % cpp + 1) * 128]
                for c in range(8):
                    p.op("PE", lambda e, c=c, wap=wap, bank=bank, sl=sl: e.matmul(
                        pfB[bank][:, 0:512], wap(c), xnb[sl][:, c, :],
                        start=(c == 0), stop=(c == 7)),
                        reads=[wkey, kxn], writes=[kbank], tag="proj")
                eb = ext[j % 3]
                keb = f"ext{j % 3}"
                sidx = {"q": 0, "k": 6, "v": 12}[name] + j
                p.op("DVE", lambda e, eb=eb, sidx=sidx: e.tensor_copy(
                    eb[:, 0:3], stash[:, sidx, 0:3]),
                    reads=["stash"], writes=[keb], tag="halo")
                p.op("ACT", lambda e, eb=eb, bank=bank: e.copy(eb[:, 3:515], pfB[bank][:, 0:512]),
                     reads=[kbank], writes=[keb], tag="rawev")
                p.op("DVE", lambda e, eb=eb, sidx=sidx: e.tensor_copy(
                    stash[:, sidx, 0:3], eb[:, 512:515]),
                    reads=[keb], writes=["stash"], tag="stashw")
                co = cvo[j % 2]
                kco = f"cvo{j % 2}"
                kcw = "c" + name + "s"
                p.op("DVE", lambda e, eb=eb, cw=cw, j=j, co=co: e.tensor_scalar(
                    co[:], eb[:, 0:512], cw[:, j, 0:1], None, ALU.mult),
                    reads=[keb, kcw], writes=[kco], tag="cv0")
                for tap in range(1, 4):
                    p.op("DVE", lambda e, eb=eb, cw=cw, j=j, co=co, tap=tap: e.scalar_tensor_tensor(
                        co[:], eb[:, tap:tap + 512], cw[:, j, tap:tap + 1], co[:],
                        ALU.mult, ALU.add),
                        reads=[keb, kcw, kco], writes=[kco], tag="cvt")
                p.op("ACT", lambda e, co=co, dst=dst, j=j: e.activation(
                    dst[:, j, :], co[:], AF.Silu),
                    reads=[kco], writes=[kdst], tag="silu")
        for pi in range(3):
            buf = wvg[pi % 2]
            kb = f"wvg{pi % 2}"
            p.dma("GQ", lambda e, pi=pi, buf=buf: e.dma_start(
                buf[:], wg_d.rearrange("(c p) q -> p c q", p=128)[:, :, pi * 512:(pi + 1) * 512]),
                writes=[kb], tag="wgld")
            for tc in range(4):
                bank = (pi * 4 + tc) % 2 + 2
                kbank = f"pB{bank}"
                for c in range(8):
                    p.op("PE", lambda e, c=c, tc=tc, buf=buf, bank=bank, sl=sl: e.matmul(
                        pfB[bank][:, 0:512], xnb[sl][:, c, tc * 128:(tc + 1) * 128],
                        buf[:, c, :], start=(c == 0), stop=(c == 7)),
                        reads=[kb, kxn], writes=[kbank], tag="gproj")
                p.op("ACT", lambda e, tc=tc, pi=pi, bank=bank: e.activation(
                    gt_[:, tc, pi * 512:(pi + 1) * 512], pfB[bank][:, 0:512], AF.Silu),
                    reads=[kbank], writes=["gt"], tag="gsilu")
        for name, dst, kdst in [("q", qs_, "qs"), ("k", ks_, "ks")]:
            for j in range(6):
                bank = 4 + (j % 2)
                kbank = f"pB{bank}"
                rb, krb = rno[j % 2], f"rno{j % 2}"
                p.op("DVE", lambda e, dst=dst, j=j: e.tensor_mul(sqv[:], dst[:, j, :], dst[:, j, :]),
                     reads=[kdst], writes=["sqv"], tag="l2sq")
                p.op("PE", lambda e, bank=bank: e.matmul(pfB[bank][:, 0:512], e64[:], sqv[:],
                                                         start=True, stop=True),
                     reads=["sqv", "e64"], writes=[kbank], tag="l2mm")
                p.op("ACT", lambda e, bank=bank, rb=rb: e.activation(
                    rb[:], pfB[bank][:, 0:512], AF.Ln, bias=eps12[:]),
                    reads=[kbank, "eps12"], writes=[krb], tag="l2ln")
                p.op("ACT", lambda e, rb=rb: e.activation(rb[:], rb[:], AF.Exp, scale=-0.5),
                     reads=[krb], writes=[krb], tag="l2ex")
                p.op("DVE", lambda e, dst=dst, j=j, rb=rb: e.tensor_mul(
                    dst[:, j, :], dst[:, j, :], rb[:]),
                    reads=[kdst, krb], writes=[kdst], tag="l2mul")

    def Kf(h, c):
        return ks_[(h % 2) * 64:(h % 2) * 64 + 64, h // 2, c * 128:(c + 1) * 128]

    def Qf(h, c):
        return qs_[(h % 2) * 64:(h % 2) * 64 + 64, h // 2, c * 128:(c + 1) * 128]

    def Sm(h):
        return S_[(h % 2) * 64:(h % 2) * 64 + 64, h // 2, :]

    def Sb(h):
        return Sbf[(h % 2) * 64:(h % 2) * 64 + 64, h // 2, :]

    def idb64(h):
        o = (h % 2) * 64
        return identb[o:o + 64, o:o + 64]

    def group_steps(t, c, g):
        gc = t * 4 + c
        gi = gc * 3 + g
        sl = gi % 2
        sl3 = gi % 3
        P = [2, 4, 0][sl3]
        X = P + 1
        kP = f"pB{P}"
        kX = f"pB{X}"
        pbx = pb0 if sl == 0 else pb1
        kpbx = f"pb{sl}"
        heads = [4 * g, 4 * g + 2, 4 * g + 1, 4 * g + 3]
        dh = dbg and (t == 0) and (c == 1) and (g == 2)
        dh0 = dbg and (t == 0) and (c == 0) and (g == 2)
        pro = []

        def s_crows():
            if g == 2 and gc + 1 < NCH:
                p.dma("SQ", lambda e, gc=gc: e.dma_start(
                    crows[(gc + 1) % 2][:], crows_d[gc + 1].rearrange("h t -> (h t)").unsqueeze(0)),
                    reads=["crows_done"], writes=[f"crows{(gc + 1) % 2}"], tag="crold")
        pro.append(s_crows)

        def s_ccol():
            for q, h in enumerate(heads):
                p.op("PE", lambda e, q=q, h=h: e.matmul(
                    pfB[P][:, q * 128:(q + 1) * 128], onesrow[:], crows[gc % 2][0:1, h * 128:(h + 1) * 128],
                    start=(q == 0), stop=False),
                    reads=[f"crows{gc % 2}", "onesrow"], writes=[kP], tag="ccol", pe_rows=(0, 1))
                p.op("PE", lambda e, q=q, h=h: e.matmul(
                    pfB[P][:, q * 128:(q + 1) * 128], crows[gc % 2][0:1, h * 128:(h + 1) * 128],
                    negonesrow[:], start=False, stop=(q == 3)),
                    reads=[f"crows{gc % 2}", "negonesrow"], writes=[kP], tag="ccol2", pe_rows=(0, 1))
            p.op("DVE", lambda e: e.tensor_add(e1lg[:], pfB[P][:, 0:512], masks4[:]),
                 reads=[kP, "masks4"], writes=["e1lg"], tag="mask")
            p.op("ACT", lambda e: e.activation(e1s[sl][:], e1lg[:], AF.Exp),
                 reads=["e1lg"], writes=[f"e1s{sl}"], tag="e1exp")
            p.op("POOL", lambda e: e.tensor_add(e1i[sl][:], e1s[sl][:], i4[:]),
                 reads=[f"e1s{sl}", "i4"], writes=[f"e1i{sl}"], tag="e1i")
        pro.append(s_ccol)

        def s_graw():
            for q, h in enumerate(heads):
                p.op("PE", lambda e, q=q, h=h: e.matmul(
                    pfB[P][:, q * 128:(q + 1) * 128], Kf(h, c), Kf(h, c),
                    start=True, stop=True),
                    reads=["ks"], writes=[kP], tag="graw",
                    pe_rows=((h % 2) * 64, (h % 2) * 64 + 64))
            p.op("DVE", lambda e: e.tensor_mul(sB0[sl][:], pfB[P][:, 0:512], e1s[sl][:]),
                 reads=[kP, f"e1s{sl}"], writes=[f"sB0{sl}"], tag="b0a")
            for q, h in enumerate(heads):
                p.op("DVE", lambda e, q=q, h=h: e.tensor_scalar(
                    sB0[sl][:, q * 128:(q + 1) * 128], sB0[sl][:, q * 128:(q + 1) * 128],
                    be_res[:, gc, h:h + 1], -1.0, ALU.mult, ALU.mult),
                    reads=[f"sB0{sl}", "be_res"], writes=[f"sB0{sl}"], tag="b0b")
            if dh:
                p.dma("SQ", lambda e: e.dma_start(dbg_b0, sB0[sl][:]), reads=[f"sB0{sl}"])
        pro.append(s_graw)

        def s_wraw():
            for q, h in enumerate(heads):
                p.op("PE", lambda e, q=q, h=h: e.matmul(
                    pfB[P][:, q * 128:(q + 1) * 128], Kf(h, c), Qf(h, c),
                    start=True, stop=True),
                    reads=["ks", "qs"], writes=[kP], tag="wraw",
                    pe_rows=((h % 2) * 64, (h % 2) * 64 + 64))
            p.op("DVE", lambda e: e.tensor_mul(sWd[sl][:], pfB[P][:, 0:512], e1i[sl][:]),
                 reads=[kP, f"e1i{sl}"], writes=[f"sWd{sl}"], tag="wd")
        pro.append(s_wraw)

        def s_vt():
            for q, h in enumerate(heads):
                p.op("PE", lambda e, q=q, h=h: e.transpose(
                    pbx[:, q * 128:(q + 1) * 128], vs_[:, h, c * 128:(c + 1) * 128], identb[:]),
                    reads=["vs", "identb"], writes=[kpbx], tag="vt")
            p.op("DVE", lambda e: e.tensor_copy(sU[sl3][:], pbx[:, 0:512]),
                 reads=[kpbx], writes=[f"sU{sl3}"], tag="u")
        pro.append(s_vt)

        def s_ktm():
            for q, h in enumerate(heads):
                p.op("PE", lambda e, q=q, h=h: e.transpose(
                    pbx[:, 512 + q * 64:512 + (q + 1) * 64], Kf(h, c), idb64(h)),
                    reads=["ks", "identb"], writes=[kpbx], tag="ktm",
                    pe_rows=((h % 2) * 64, (h % 2) * 64 + 64))
            p.op("DVE", lambda e: e.tensor_copy(
                rcols[:, sl * 4:sl * 4 + 4],
                e1i[sl][:].rearrange("p (q t) -> p q t", q=4)[:, :, 127:128].squeeze(2)),
                reads=[f"e1i{sl}"], writes=[f"rcols{sl}"], tag="rcol")
            for q, h in enumerate(heads):
                p.op("DVE", lambda e, q=q, h=h: e.tensor_scalar(
                    khat[sl][:, q * 64:(q + 1) * 64], pbx[:, 512 + q * 64:512 + (q + 1) * 64],
                    rcols[:, sl * 4 + q:sl * 4 + q + 1], None, ALU.mult),
                    reads=[kpbx, f"rcols{sl}"], writes=[f"khat{sl}"], tag="khat")
            for q, h in enumerate(heads):
                p.op("DVE", lambda e, q=q, h=h: e.tensor_scalar(
                    wtf32[sl][:, q * 64:(q + 1) * 64], pbx[:, 512 + q * 64:512 + (q + 1) * 64],
                    b_res[:, gc, h:h + 1], None, ALU.mult),
                    reads=[kpbx, "b_res"], writes=[f"wtf32{sl}"], tag="kbb")
            p.op("DVE", lambda e: e.tensor_copy(skbb[sl][0][:], wtf32[sl][:]),
                 reads=[f"wtf32{sl}"], writes=[f"skbb{sl}0"], tag="kbbh")
            p.op("DVE", lambda e: e.scalar_tensor_tensor(
                skbb[sl][1][:], wtf32[sl][:], 1.0, skbb[sl][0][:], ALU.mult, ALU.subtract),
                reads=[f"wtf32{sl}", f"skbb{sl}0"], writes=[f"skbb{sl}1"], tag="kbbl")
            if dh0:
                p.dma("SQ", lambda e: e.dma_start(dbg_khat0, khat[sl][:]), reads=[f"khat{sl}"])
            if dh:
                p.dma("SQ", lambda e: e.dma_start(dbg_kbb[:, 0, :], skbb[sl][0][:]), reads=[f"skbb{sl}0"])
                p.dma("SQ", lambda e: e.dma_start(dbg_kbb[:, 1, :], skbb[sl][1][:]), reads=[f"skbb{sl}1"])
                p.dma("SQ", lambda e: e.dma_start(dbg_khat, khat[sl][:]), reads=[f"khat{sl}"])
        pro.append(s_ktm)

        def s_a0():
            for q in range(4):
                p.op("PE", lambda e, q=q: e.transpose(
                    pbx[:, q * 128:(q + 1) * 128], sB0[sl][:, q * 128:(q + 1) * 128], identb[:]),
                    reads=[f"sB0{sl}", "identb"], writes=[kpbx], tag="a0")
            p.op("ACT", lambda e: e.copy(sA0[sl][:], pbx[:, 0:512]),
                 reads=[kpbx], writes=[f"sA0{sl}"], tag="a0ev")
        pro.append(s_a0)

        body = []
        Acur = (sA0[sl], f"sA0{sl}")
        Bcur = (sB0[sl], f"sB0{sl}")
        Dcur = (sU[sl3], f"sU{sl3}")
        Wcur = (skbb[sl], f"skbb{sl}")

        def mk_level(j, Acur, Bcur, Dcur, Wcur):
            Ain, kAin = Acur
            Bin, kBin = Bcur
            Din, kDin = Dcur
            Win, kWin = Wcur
            Aout, kAout = sAx[sl][j % 2], f"sAx{sl}_{j % 2}"
            Bout, kBout = sBx[sl][j % 2], f"sBx{sl}_{j % 2}"
            Dout, kDout = sDx[sl], f"sDx{sl}_"
            Wout, kWout = sWx[sl], f"sWx{sl}_"

            def step():
                if j == 0:
                    p.op("POOL", lambda e: e.tensor_add(sBt[sl][:], Bin[:], i4[:]),
                         reads=[kBin, "i4"], writes=[f"sBt{sl}"], tag="bt")
                if j == 0:
                    for q in range(4):
                        p.op("PE", lambda e, q=q: e.matmul(
                            pfB[X][:, q * 128:(q + 1) * 128], sBt[sl][:, q * 128:(q + 1) * 128],
                            Din[:, q * 128:(q + 1) * 128], start=True, stop=True),
                            reads=[f"sBt{sl}", kDin], writes=[kX], tag="apl")
                else:
                    for q in range(4):
                        p.op("PE", lambda e, q=q: e.matmul(
                            pfB[X][:, q * 128:(q + 1) * 128], sBt[sl][:, q * 128:(q + 1) * 128],
                            Din[0][:, q * 128:(q + 1) * 128], start=True, stop=False),
                            reads=[f"sBt{sl}", kDin + "0"], writes=[kX], tag="apl")
                        p.op("PE", lambda e, q=q: e.matmul(
                            pfB[X][:, q * 128:(q + 1) * 128], sBt[sl][:, q * 128:(q + 1) * 128],
                            Din[1][:, q * 128:(q + 1) * 128], start=False, stop=True),
                            reads=[f"sBt{sl}", kDin + "1"], writes=[kX], tag="apl2")
                dst, kdst = Dout, kDout
                p.op("ACT", lambda e, dst=dst: e.copy(dst[0][:], pfB[X][:, 0:512]),
                     reads=[kX], writes=[kdst + "0"], tag="dev")
                p.op("DVE", lambda e, dst=dst: e.scalar_tensor_tensor(
                    dst[1][:], pfB[X][:, 0:512], 1.0, dst[0][:], ALU.mult, ALU.subtract),
                    reads=[kX, kdst + "0"], writes=[kdst + "1"], tag="devl")
                for q in range(4):
                    p.op("PE", lambda e, q=q: e.matmul(
                        pfB[X][:, q * 64:(q + 1) * 64], sBt[sl][:, q * 128:(q + 1) * 128],
                        Win[0][:, q * 64:(q + 1) * 64], start=True, stop=False),
                        reads=[f"sBt{sl}", kWin + "0"], writes=[kX], tag="awk")
                    p.op("PE", lambda e, q=q: e.matmul(
                        pfB[X][:, q * 64:(q + 1) * 64], sBt[sl][:, q * 128:(q + 1) * 128],
                        Win[1][:, q * 64:(q + 1) * 64], start=False, stop=True),
                        reads=[f"sBt{sl}", kWin + "1"], writes=[kX], tag="awk2")
                if j < 6:
                    p.op("ACT", lambda e: e.copy(Wout[0][:], pfB[X][:, 0:256]),
                         reads=[kX], writes=[kWout + "0"], tag="wevh")
                    p.op("DVE", lambda e: e.scalar_tensor_tensor(
                        Wout[1][:], pfB[X][:, 0:256], 1.0, Wout[0][:], ALU.mult, ALU.subtract),
                        reads=[kX, kWout + "0"], writes=[kWout + "1"], tag="wevl")
                else:
                    p.op("DVE", lambda e: e.tensor_scalar(
                        Wout[0][:], pfB[X][:, 0:256], -1.0, None, ALU.mult),
                        reads=[kX], writes=[kWout + "0"], tag="wkfh")
                    p.op("DVE", lambda e: e.scalar_tensor_tensor(
                        Wout[1][:], pfB[X][:, 0:256], -1.0, Wout[0][:], ALU.mult, ALU.subtract),
                        reads=[kX, kWout + "0"], writes=[kWout + "1"], tag="wkfl")
                    if dh:
                        p.dma("SQ", lambda e: e.dma_start(dbg_wkt[:, 0, :], Wout[0][:]), reads=[kWout + "0"])
                        p.dma("SQ", lambda e: e.dma_start(dbg_wkt[:, 1, :], Wout[1][:]), reads=[kWout + "1"])
                        p.dma("SQ", lambda e: e.dma_start(dbg_dv[:, 0, :], sDx[sl][0][:]), reads=[f"sDx{sl}_0"])
                        p.dma("SQ", lambda e: e.dma_start(dbg_dv[:, 1, :], sDx[sl][1][:]), reads=[f"sDx{sl}_1"])
                if j < 6:
                    for q in range(4):
                        p.op("PE", lambda e, q=q: e.matmul(
                            pfB[P][:, q * 128:(q + 1) * 128], Bin[:, q * 128:(q + 1) * 128],
                            Ain[:, q * 128:(q + 1) * 128], start=True, stop=True),
                            reads=[kBin, kAin], writes=[kP], tag="sqa")
                    p.op("ACT", lambda e: e.copy(Aout[:], pfB[P][:, 0:512]),
                         reads=[kP], writes=[kAout], tag="aev")
                    for q in range(4):
                        p.op("PE", lambda e, q=q: e.matmul(
                            pfB[X][:, q * 128:(q + 1) * 128], Ain[:, q * 128:(q + 1) * 128],
                            Bin[:, q * 128:(q + 1) * 128], start=True, stop=True),
                            reads=[kAin, kBin], writes=[kX], tag="sqb")
                    p.op("DVE", lambda e: e.tensor_copy(Bout[:], pfB[X][:, 0:512]),
                         reads=[kX], writes=[kBout], tag="bev")
                    p.op("DVE", lambda e: e.tensor_add(sBt[sl][:], pfB[X][:, 0:512], i4[:]),
                         reads=[kX, "i4"], writes=[f"sBt{sl}"], tag="btev")
            return step

        for j in range(7):
            body.append(mk_level(j, Acur, Bcur, Dcur, Wcur))
            if j < 6:
                Acur = (sAx[sl][j % 2], f"sAx{sl}_{j % 2}")
                Bcur = (sBx[sl][j % 2], f"sBx{sl}_{j % 2}")
                Dcur = (sDx[sl], f"sDx{sl}_")
                Wcur = (sWx[sl], f"sWx{sl}_")

        def s_wkfm():
            for hl in range(2):
                for q, h in enumerate(heads):
                    o = (h % 2) * 64
                    p.op("PE", lambda e, q=q, h=h, o=o, hl=hl: e.transpose(
                        pbx[o:o + 64, hl * 512 + q * 128:hl * 512 + (q + 1) * 128],
                        sWx[sl][hl][:, q * 64:(q + 1) * 64], identb[:]),
                        reads=[f"sWx{sl}_{hl}", "identb"], writes=[kpbx], tag="wkfm")
            for hl in range(2):
                p.op("DVE", lambda e, hl=hl: e.tensor_copy(
                    sWkF[sl][hl][0:64, 0:256], pbx[0:64, hl * 512:hl * 512 + 256]),
                    reads=[kpbx], writes=[f"sWkF{sl}{hl}"], tag="wkfev")
                p.op("DVE", lambda e, hl=hl: e.tensor_copy(
                    sWkF[sl][hl][64:128, 0:256], pbx[64:128, hl * 512 + 256:hl * 512 + 512]),
                    reads=[kpbx], writes=[f"sWkF{sl}{hl}"], tag="wkfev2")
            if dh:
                for hl in range(2):
                    p.dma("SQ", lambda e, hl=hl: e.dma_start(dbg_wkf[:, hl, :], sWkF[sl][hl][:]), reads=[f"sWkF{sl}{hl}"])
        body.append(s_wkfm)

        def s_r2a():
            p.op("PE", lambda e: e.matmul(pfB[X][:, 0:512], identb[:], sDx[sl][0][:],
                                          start=True, stop=False),
                 reads=["identb", f"sDx{sl}_0"], writes=[kX], tag="r2i0")
            p.op("PE", lambda e: e.matmul(pfB[X][:, 0:512], identb[:], sDx[sl][1][:],
                                          start=False, stop=False),
                 reads=["identb", f"sDx{sl}_1"], writes=[kX], tag="r2i1")
            for q, h in enumerate(heads):
                o = (h % 2) * 64
                last = (q == 3)
                p.op("PE", lambda e, q=q, h=h, o=o: e.matmul(
                    pfB[X][:, q * 128:(q + 1) * 128], sWkF[sl][0][o:o + 64, (q % 2) * 128:(q % 2 + 1) * 128],
                    Sb(h), start=False, stop=False),
                    reads=[f"sWkF{sl}0", "Sbf"], writes=[kX], tag="wks",
                    pe_rows=(o, o + 64))
                p.op("PE", lambda e, q=q, h=h, o=o, last=last: e.matmul(
                    pfB[X][:, q * 128:(q + 1) * 128], sWkF[sl][1][o:o + 64, (q % 2) * 128:(q % 2 + 1) * 128],
                    Sb(h), start=False, stop=last),
                    reads=[f"sWkF{sl}1", "Sbf"], writes=[kX], tag="wks2",
                    pe_rows=(o, o + 64))
            for q, h in enumerate(heads):
                p.op("DVE", lambda e, q=q, h=h: e.tensor_scalar(
                    sU[sl3][:, q * 128:(q + 1) * 128], pfB[X][:, q * 128:(q + 1) * 128],
                    be_res[:, gc, h:h + 1], None, ALU.mult),
                    reads=[kX, "be_res"], writes=[f"sU{sl3}"], tag="dfc")
            if dh:
                p.dma("SQ", lambda e: e.dma_start(dbg_df, sU[sl3][:]), reads=[f"sU{sl3}"])
                p.dma("SQ", lambda e: e.dma_start(dbg_sbf, Sbf[:]), reads=["Sbf"])
            if dh0:
                p.dma("SQ", lambda e: e.dma_start(dbg_df0, sU[sl3][:]), reads=[f"sU{sl3}"])
        body.append(s_r2a)

        def s_out():
            for q, h in enumerate(heads):
                p.op("PE", lambda e, q=q, h=h: e.matmul(
                    pfB[X][:, q * 128:(q + 1) * 128], Qf(h, c), Sb(h),
                    start=True, stop=True),
                    reads=["qs", "Sbf"], writes=[kX], tag="oint",
                    pe_rows=((h % 2) * 64, (h % 2) * 64 + 64))
            for q in range(4):
                p.op("PE", lambda e, q=q: e.matmul(
                    pfB[P][:, q * 128:(q + 1) * 128], sWd[sl][:, q * 128:(q + 1) * 128],
                    sU[sl3][:, q * 128:(q + 1) * 128], start=True, stop=True),
                    reads=[f"sWd{sl}", f"sU{sl3}"], writes=[kP], tag="ointra")
            for q, h in enumerate(heads):
                p.op("DVE", lambda e, q=q, h=h: e.tensor_scalar(
                    oS[sl][:, q * 128:(q + 1) * 128], pfB[X][:, q * 128:(q + 1) * 128],
                    b_res[:, gc, h:h + 1], None, ALU.mult),
                    reads=[kX, "b_res"], writes=[f"oS{sl}"], tag="oev1")
            p.op("DVE", lambda e: e.tensor_add(oS[sl][:], oS[sl][:], pfB[P][:, 0:512]),
                 reads=[kP, f"oS{sl}"], writes=[f"oS{sl}"], tag="oev2")
        body.append(s_out)

        def s_gate():
            p.op("DVE", lambda e: e.tensor_mul(o2S[sl][:], oS[sl][:], oS[sl][:]),
                 reads=[f"oS{sl}"], writes=[f"o2S{sl}"], tag="gsq")
            p.op("DVE", lambda e: e.reduce_sum(
                rsts[:, sl * 8:sl * 8 + 4],
                o2S[sl][:].rearrange("p (q v) -> p q v", q=4), axis=AX),
                reads=[f"o2S{sl}"], writes=[f"rst{sl}"], tag="gred")
            p.op("ACT", lambda e: e.activation(rsts[:, sl * 8 + 4:sl * 8 + 8],
                                               rsts[:, sl * 8:sl * 8 + 4], AF.Ln,
                                               bias=epsln[:], scale=1.0 / DV),
                 reads=[f"rst{sl}", "epsln"], writes=[f"rstl{sl}"], tag="gln")
            p.op("ACT", lambda e: e.activation(rsts[:, sl * 8 + 4:sl * 8 + 8],
                                               rsts[:, sl * 8 + 4:sl * 8 + 8], AF.Exp, scale=-0.5),
                 reads=[f"rstl{sl}"], writes=[f"rstl{sl}"], tag="gex")
            for q in range(4):
                p.op("DVE", lambda e, q=q: e.tensor_scalar(
                    oS[sl][:, q * 128:(q + 1) * 128], oS[sl][:, q * 128:(q + 1) * 128],
                    rsts[:, sl * 8 + 4 + q:sl * 8 + 5 + q], None, ALU.mult),
                    reads=[f"oS{sl}", f"rstl{sl}"], writes=[f"oS{sl}"], tag="grs")
            p.op("POOL", lambda e: e.tensor_mul(o2S[sl][:], oS[sl][:], gnw4[:]),
                 reads=[f"oS{sl}", "gnw4"], writes=[f"o2S{sl}"], tag="ggnw")
            for q, h in enumerate(heads):
                p.op("DVE", lambda e, q=q, h=h: e.tensor_mul(
                    ogt[sl][:, q * 128:(q + 1) * 128], o2S[sl][:, q * 128:(q + 1) * 128],
                    gt_[:, c, h * 128:(h + 1) * 128]),
                    reads=[f"o2S{sl}", "gt"], writes=[f"ogt{sl}"], tag="gmul")
        body.append(s_gate)

        def s_ogtp():
            for q in range(4):
                p.op("PE", lambda e, q=q: e.transpose(
                    pbx[:, q * 128:(q + 1) * 128], ogt[sl][:, q * 128:(q + 1) * 128],
                    identb[:]),
                    reads=[f"ogt{sl}", "identb"], writes=[kpbx], tag="ogtp")
            for q in range(4):
                d = heads[q]
                p.op("DVE", lambda e, q=q, d=d: e.tensor_copy(
                    ofm[:, d, c * 128:(c + 1) * 128], pbx[:, q * 128:(q + 1) * 128]),
                    reads=[kpbx], writes=["ofm"], tag="ogev")
        body.append(s_ogtp)

        def s_state():
            if g == 0:
                p.op("ACT", lambda e, gc=gc: e.activation(
                    bcrow[:],
                    crows[gc % 2][0:1, 0:NH * 128].rearrange("p (h t) -> p h t", h=NH)[:, :, 127:128].squeeze(2),
                    AF.Exp),
                    reads=[f"crows{gc % 2}"], writes=["bcrow"], tag="bc1")
                p.op("PE", lambda e: e.matmul(pfB[X][:, 0:NH], onesrow[:], bcrow[:],
                                              start=True, stop=True),
                     reads=["bcrow", "onesrow"], writes=[kX], tag="bc2", pe_rows=(0, 1))
                p.op("DVE", lambda e: e.tensor_copy(bccol[:], pfB[X][:, 0:NH]),
                     reads=[kX], writes=["bccol"], tag="bc3")
            for q, h in enumerate(heads):
                o = (h % 2) * 64
                p.op("PE", lambda e, q=q, h=h, o=o: e.matmul(
                    pfB[P][o:o + 64, q * 128:(q + 1) * 128], khat[sl][:, q * 64:(q + 1) * 64],
                    sU[sl3][:, q * 128:(q + 1) * 128], start=True, stop=True),
                    reads=[f"khat{sl}", f"sU{sl3}"], writes=[kP], tag="smm")
            for q, h in enumerate(heads):
                o = (h % 2) * 64
                p.op("DVE", lambda e, q=q, h=h, o=o: e.scalar_tensor_tensor(
                    Sb(h), Sm(h), bccol[o:o + 64, h:h + 1],
                    pfB[P][o:o + 64, q * 128:(q + 1) * 128], ALU.mult, ALU.add),
                    reads=["S", "bccol", kP], writes=["Sbf"], tag="subf")
            for q, h in enumerate(heads):
                o = (h % 2) * 64
                p.op("DVE", lambda e, q=q, h=h, o=o: e.scalar_tensor_tensor(
                    Sm(h), Sm(h), bccol[o:o + 64, h:h + 1],
                    pfB[P][o:o + 64, q * 128:(q + 1) * 128], ALU.mult, ALU.add),
                    reads=["S", "bccol", kP], writes=["S"], tag="sup")
            if dh0:
                p.dma("SQ", lambda e: e.dma_start(dbg_sbf0, Sbf[:]), reads=["Sbf"])
                p.dma("SQ", lambda e: e.dma_start(dbg_bcc, bccol[:]), reads=["bccol"])
        body.append(s_state)
        return pro, body


    def emit_rec(t, D=7):
        allsteps = []
        cnt = 0
        for c in range(4):
            for g in range(3):
                if cnt >= rec_limit:
                    break
                cnt += 1
                pro, body = group_steps(t, c, g)
                allsteps.append(pro + body)
        seq = []
        total = D * (len(allsteps) - 1) + max(len(st) for st in allsteps) if allsteps else 0
        for slot in range(total):
            for k in range(len(allsteps)):
                idx = slot - D * k
                if 0 <= idx < len(allsteps[k]):
                    seq.append((t * 12 + k, idx, allsteps[k][idx]))
        for gi_, idx_, st in seq[:rec_steps]:
            p.cur_lid = (gi_, idx_)
            st()
        p.cur_lid = None

    def emit_backend(t):
        p.dma("SQ", lambda e, t=t: e.dma_start(sgb[:], sg_d[t]),
              reads=["sg_done"], writes=["sgb"], tag="sgld")
        p.dma("SQ", lambda e, t=t: e.dma_start(
            xre[:], x_d.rearrange("(s p) d -> p s d", p=128)[:, t * 4:(t + 1) * 4, :]),
            writes=["xre"], tag="xrld")
        for ehalf in range(2):
            for pi in range(4):
                buf = wo_s[pi % 2]
                kb = f"wo{pi % 2}"
                p.dma("GQ", lambda e, pi=pi, buf=buf: e.dma_start(
                    buf[:], wo_d.rearrange("(c p) q -> p c q", p=128)[:, pi * 3:(pi + 1) * 3, :]),
                    writes=[kb], tag="wold")
                for le in range(4):
                    e_ = ehalf * 4 + le
                    for ld in range(3):
                        d = pi * 3 + ld
                        p.op("PE", lambda e, e_=e_, d=d, ld=ld, buf=buf, le=le: e.matmul(
                            pfB[le][:, 0:512], buf[:, ld, e_ * 128:(e_ + 1) * 128], ofm[:, d, :],
                            start=(d == 0), stop=(d == 11)),
                            reads=[kb, "ofm"], writes=[f"pB{le}"], tag="womm")
            for le in range(4):
                e_ = ehalf * 4 + le
                kbank = f"pB{le}"
                krn = f"rno{le % 2}"
                p.op("DVE", lambda e, e_=e_, le=le: e.tensor_mul(
                    rno[le % 2][:], pfB[le][:, 0:512], sgb[:, e_, :]),
                    reads=[kbank, "sgb"], writes=[krn], tag="smul")
                for tc in range(4):
                    p.op("PE", lambda e, le=le, tc=tc: e.transpose(
                        pfB[4 + le % 2][:, tc * 128:(tc + 1) * 128],
                        rno[le % 2][:, tc * 128:(tc + 1) * 128], identf[:]),
                        reads=[krn, "identf"], writes=[f"pB{4 + le % 2}"], tag="ctb")
                for tc in range(4):
                    p.op("DVE", lambda e, le=le, e_=e_, tc=tc: e.tensor_add(
                        xre[:, tc, e_ * 128:(e_ + 1) * 128],
                        pfB[4 + le % 2][:, tc * 128:(tc + 1) * 128],
                        xre[:, tc, e_ * 128:(e_ + 1) * 128]),
                        reads=[f"pB{4 + le % 2}", "xre"], writes=["xre"], tag="comb")
        p.dma("SQ", lambda e, t=t: e.dma_start(
            y_d.rearrange("(s p) d -> p s d", p=128)[:, t * 4:(t + 1) * 4, :], xre[:]),
            reads=["xre"], writes=["xre"], tag="yst")

    if phase == "A":
        p.emit()
        return nc, p
    if dbg:
        dbg_kbb = nc.dram_tensor("dbg_kbb", [128, 2, 256], BF16, kind="Internal").ap()
        dbg_khat = nc.dram_tensor("dbg_khat", [128, 256], BF16, kind="Internal").ap()
        dbg_b0 = nc.dram_tensor("dbg_b0", [128, 512], BF16, kind="Internal").ap()
        dbg_dv = nc.dram_tensor("dbg_dv", [128, 2, 512], BF16, kind="Internal").ap()
        dbg_wkt = nc.dram_tensor("dbg_wkt", [128, 2, 256], BF16, kind="Internal").ap()
        dbg_wkf = nc.dram_tensor("dbg_wkf", [128, 2, 256], BF16, kind="Internal").ap()
        dbg_df = nc.dram_tensor("dbg_df", [128, 512], BF16, kind="Internal").ap()
        dbg_sbf = nc.dram_tensor("dbg_sbf", [128, 6, 128], BF16, kind="Internal").ap()
        dbg_sbf0 = nc.dram_tensor("dbg_sbf0", [128, 6, 128], BF16, kind="Internal").ap()
        dbg_khat0 = nc.dram_tensor("dbg_khat0", [128, 256], BF16, kind="Internal").ap()
        dbg_df0 = nc.dram_tensor("dbg_df0", [128, 512], BF16, kind="Internal").ap()
        dbg_bcc = nc.dram_tensor("dbg_bcc", [128, 12], F32, kind="Internal").ap()
        dbg_q = nc.dram_tensor("dbg_q", [128, 6, TB], BF16, kind="Internal").ap()
        dbg_k = nc.dram_tensor("dbg_k", [128, 6, TB], BF16, kind="Internal").ap()
        dbg_v = nc.dram_tensor("dbg_v", [128, 12, TB], BF16, kind="Internal").ap()
        dbg_g = nc.dram_tensor("dbg_g", [128, 4, VD], BF16, kind="Internal").ap()
        dbg_ofm = nc.dram_tensor("dbg_ofm", [128, 12, TB], BF16, kind="Internal").ap()

    emit_frontend(0)
    if dbg:
        p.dma("SQ", lambda e: e.dma_start(dbg_q, qs_[:]), reads=["qs"])
        p.dma("SQ", lambda e: e.dma_start(dbg_k, ks_[:]), reads=["ks"])
        p.dma("SQ", lambda e: e.dma_start(dbg_v, vs_[:]), reads=["vs"])
        p.dma("SQ", lambda e: e.dma_start(dbg_g, gt_[:]), reads=["gt"])
    if phase == "fe":
        p.emit()
        return nc, p
    for t in range(NTB):
        emit_rec(t)
        if phase == "rec1" and t == 0:
            break
        if t + 1 < NTB:
            emit_frontend(t + 1)
        emit_backend(t)
        if dbg and t == 0:
            p.dma("SQ", lambda e: e.dma_start(dbg_ofm, ofm[:]), reads=["ofm"])

    p.emit()
    return nc, p


_BF = ml_dtypes.bfloat16


def _consts():
    ii = np.arange(128)
    identf = np.eye(128, dtype=np.float32)
    identb = np.eye(128).astype(_BF)
    e64 = ((ii[:, None] // 64) == (ii[None, :] // 64)).astype(_BF)
    utones = (ii[:, None] <= ii[None, :]).astype(np.float32)
    m1 = np.where(ii[:, None] < ii[None, :], 0.0, -1e30).astype(np.float32)
    masks4 = np.ascontiguousarray(np.tile(m1, (1, 4)).astype(_BF))
    i4 = np.ascontiguousarray(np.tile(np.eye(128, dtype=np.float32), (1, 4))).astype(_BF)
    onesrow = np.ones((1, 128), np.float32)
    negonesrow = -np.ones((1, 128), np.float32)
    return dict(identf=identf, identb=identb, e64=e64, utones=utones,
                masks4=masks4, i4=i4, onesrow=onesrow, negonesrow=negonesrow)


def kernel(**inp):
    x = np.ascontiguousarray(np.asarray(inp["L0_attn_hidden_states"], dtype=np.float32))
    B, T, _ = x.shape
    g = lambda k: np.ascontiguousarray(np.asarray(inp[k], dtype=np.float32))
    consts = _consts()
    w1 = g("L0_ffn_W1")
    w2 = g("L0_ffn_W2")
    base = dict(
        wq=g("L0_attn_Wq"), wk=g("L0_attn_Wk"), wv=g("L0_attn_Wv"), wg=g("L0_attn_Wg"),
        wgkb=np.ascontiguousarray(np.concatenate([g("L0_attn_Wgk"), g("L0_attn_Wb")], axis=1)),
        w1t=np.ascontiguousarray(w1.T), w2t=np.ascontiguousarray(w2.T),
        wo=g("L0_attn_Wo"),
        b1=np.ascontiguousarray(g("L0_ffn_b1").reshape(32, 128).T),
        b2=np.ascontiguousarray(g("L0_ffn_b2").reshape(8, 128).T),
        cq=np.ascontiguousarray(g("L0_attn_conv_q").reshape(6, 128, 4).transpose(1, 0, 2)),
        ck=np.ascontiguousarray(g("L0_attn_conv_k").reshape(6, 128, 4).transpose(1, 0, 2)),
        cv=np.ascontiguousarray(g("L0_attn_conv_v").reshape(12, 128, 4).transpose(1, 0, 2)),
        alog=g("L0_attn_A_log").reshape(1, NH),
        dtbb=np.ascontiguousarray(np.concatenate([g("L0_attn_dt_bias"), g("L0_attn_bb")]).reshape(1, 24)),
        gnorm=g("L0_attn_gnorm_w").reshape(1, DV),
        **consts,
    )
    nc, _p = build_program(T)
    in_maps = [dict(base, x=x[b]) for b in range(B)]
    res = run_bass_kernel_spmd(nc, in_maps, core_ids=list(range(B)))
    return np.stack([r["y"] for r in res.results]).astype(np.float32)


# version 08
# speedup vs baseline: 1.0885x; 1.0885x over previous
import numpy as np
import ml_dtypes
import concourse.bass as bass
import concourse.mybir as mybir
from concourse.bass_utils import run_bass_kernel_spmd

F32 = mybir.dt.float32
BF16 = mybir.dt.bfloat16
AX = mybir.AxisListType.X
ALU = mybir.AluOpType
AF = mybir.ActivationFunctionType

COMPUTE = ("PE", "DVE", "ACT", "POOL", "SP")
QUEUES = {"SQ": "SP", "GQ": "POOL", "AQ": "ACT"}
NQS = 8


class _Node:
    __slots__ = ("eng", "seq", "fn", "deps", "need_inc", "tag", "queue", "qseq", "lid")

    def __init__(self, eng, seq, fn, tag):
        self.eng = eng
        self.seq = seq
        self.fn = fn
        self.deps = {}
        self.need_inc = False
        self.tag = tag
        self.queue = None
        self.qseq = None


class Prog:

    def __init__(self, nc):
        self.nc = nc
        self.streams = {e: [] for e in COMPUTE}
        self.qcount = {q: 0 for q in QUEUES}
        self.bufs = {}
        self._prev_pe_rows = None
        self.cur_lid = None
        self._lidn = 0
        self.lstate = {}
        self.lbad = []

    def _audit(self, lid, reads, writes, tag):
        if lid is None:
            return
        for k in reads:
            if k.startswith("pB") or k.startswith("pb") or k.startswith("spb"):
                continue
            st = self.lstate.get(k)
            if st is None:
                self.lstate[k] = [None, lid, None]
                continue
            if st[0] is not None and st[0] > lid:
                self.lbad.append((k, "stale-read", st[0][:2], lid[:2], tag, st[2]))
            if st[1] is None or lid > st[1]:
                st[1] = lid
        for k in writes:
            if k.startswith("pB") or k.startswith("pb") or k.startswith("spb"):
                continue
            st = self.lstate.get(k)
            if st is None:
                self.lstate[k] = [lid, lid, tag]
                continue
            if st[1] is not None and st[1] > lid:
                self.lbad.append((k, "late-write", st[1][:2], lid[:2], tag, st[2]))
            if st[0] is None or lid > st[0]:
                st[0] = lid
            if st[1] is None or lid > st[1]:
                st[1] = lid
            st[2] = tag

    def _depend(self, node, ev):
        if ev is None:
            return
        dom, seq = ev
        if dom == node.eng:
            if seq >= node.seq:
                return
            if dom == "PE":
                return
        key = (dom, seq % NQS) if dom in QUEUES else dom
        if seq > node.deps.get(key, -1):
            node.deps[key] = seq

    def _touch(self, node_ev, node, reads, writes):
        for k in reads:
            ent = self.bufs.get(k)
            if ent is not None:
                self._depend(node, ent[0])
        for k in writes:
            ent = self.bufs.get(k)
            if ent is not None:
                self._depend(node, ent[0])
                for r in ent[1]:
                    self._depend(node, r)
        for k in reads:
            self.bufs.setdefault(k, [None, []])[1].append(node_ev)
        for k in writes:
            self.bufs[k] = [node_ev, []]

    def op(self, eng, fn, reads=(), writes=(), tag="", pe_rows=(0, 128)):
        seq = len(self.streams[eng])
        node = _Node(eng, seq, fn, tag)
        node.lid = None if self.cur_lid is None else (self.cur_lid[0], self.cur_lid[1], self._lidn)
        self._lidn += 1
        self.streams[eng].append(node)
        if eng == "PE":
            pr = self._prev_pe_rows
            if pr is not None and seq > 0 and (pe_rows[1] <= pr[0] or pr[1] <= pe_rows[0]):
                if node.deps.get("PE", -1) < seq - 1:
                    node.deps["PE"] = seq - 1
            self._prev_pe_rows = pe_rows
        self._touch((eng, seq), node, reads, writes)
        self._audit(node.lid, reads, writes, tag)
        return node

    def dma(self, queue, fn, reads=(), writes=(), tag=""):
        issuer = QUEUES[queue]
        seq = len(self.streams[issuer])
        node = _Node(issuer, seq, fn, tag)
        node.lid = None if self.cur_lid is None else (self.cur_lid[0], self.cur_lid[1], self._lidn)
        self._lidn += 1
        node.queue = queue
        node.qseq = self.qcount[queue]
        self.qcount[queue] += 1
        self.streams[issuer].append(node)
        self._touch((queue, node.qseq), node, reads, writes)
        self._audit(node.lid, reads, writes, tag)
        return node

    def global_barrier(self):
        last_compute = {}
        for e in COMPUTE:
            for i in range(len(self.streams[e]) - 1, -1, -1):
                if self.streams[e][i].queue is None:
                    last_compute[e] = i
                    break
        qs = dict(self.qcount)
        for e in COMPUTE:
            node = self.op(e, lambda en: en.nop(), tag="barrier")
            for e2, i2 in last_compute.items():
                if e2 != e:
                    self._depend(node, (e2, i2))
            for q, n in qs.items():
                if n > 0:
                    self._depend(node, (q, n - 1))

    def emit(self):
        if self.lbad:
            raise AssertionError("logical-order hazards: " + repr(self.lbad[:20]))
        nc = self.nc
        for e in COMPUTE:
            for node in self.streams[e]:
                for dom, seq in node.deps.items():
                    if not isinstance(dom, tuple):
                        self.streams[dom][seq].need_inc = True
        prefix = {}
        for e in COMPUTE:
            c, pl = 0, []
            for node in self.streams[e]:
                if node.need_inc:
                    c += 1
                pl.append(c)
            prefix[e] = pl
        self._prefix = prefix

        ctx, semh = [], {}
        for nm in COMPUTE:
            cm = nc.semaphore("sem_" + nm)
            ctx.append(cm)
            semh[nm] = cm.__enter__()
        for qn in QUEUES:
            for i in range(NQS):
                cm = nc.semaphore(f"sem_{qn}{i}")
                ctx.append(cm)
                semh[(qn, i)] = cm.__enter__()

        blk_cm = nc.Block()
        blk = blk_cm.__enter__()
        for e in COMPUTE:
            section = getattr(blk, {"PE": "tensor", "DVE": "vector", "ACT": "scalar",
                                    "POOL": "gpsimd", "SP": "sync"}[e])

            def body(eng_obj, e=e):
                waited = {}
                for node in self.streams[e]:
                    for key, seq in sorted(node.deps.items(), key=str):
                        if isinstance(key, tuple):
                            val = 16 * (seq // NQS + 1)
                        else:
                            val = prefix[key][seq]
                            if val == 0:
                                continue
                        if waited.get(key, -1) >= val:
                            continue
                        eng_obj.wait_ge(semh[key], val)
                        waited[key] = val
                    if node.queue is not None and node.qseq >= NQS:
                        key = (node.queue, node.qseq % NQS)
                        val = 16 * (node.qseq // NQS)
                        if waited.get(key, -1) < val:
                            eng_obj.wait_ge(semh[key], val)
                            waited[key] = val
                    inst = node.fn(eng_obj)
                    if node.queue is not None:
                        inst.then_inc(semh[(node.queue, node.qseq % NQS)], 16)
                    elif node.need_inc:
                        inst.then_inc(semh[e], 1)
                if e == "SP":
                    for q in QUEUES:
                        for i in range(NQS):
                            cnt = (self.qcount[q] - i + NQS - 1) // NQS
                            if cnt > 0:
                                eng_obj.wait_ge(semh[(q, i)], 16 * cnt)

            section(body)
        blk_cm.__exit__(None, None, None)
        for cm in ctx:
            cm.__exit__(None, None, None)

    def stats(self):
        s = {e: len(self.streams[e]) for e in COMPUTE}
        s.update({q: self.qcount[q] for q in QUEUES})
        s["incs"] = {e: (self._prefix[e][-1] if self.streams[e] else 0) for e in COMPUTE}
        return s


D = 1024
NH, DK, DV = 12, 64, 128
QD, VD, FF = 768, 1536, 4096
CH = 128
TA = 1024
TB = 512


DBG_G = 2


def build_program(T, dbg=False, phase="full", rec_limit=10**9, rec_steps=10**9):
    NTA = T // TA
    NTB = T // TB
    NCH = T // CH
    nc = bass.Bass("TRN2", target_bir_lowering=False)

    def din(name, shape, dt=F32):
        return nc.dram_tensor(name, list(shape), dt, kind="ExternalInput").ap()

    x_d = din("x", (T, D))
    wq_d = din("wq", (D, QD))
    wk_d = din("wk", (D, QD))
    wv_d = din("wv", (D, VD))
    wg_d = din("wg", (D, VD))
    wgkb_d = din("wgkb", (D, 24))
    w1t_d = din("w1t", (D, FF))
    w2t_d = din("w2t", (FF, D))
    wo_d = din("wo", (VD, D))
    b1_d = din("b1", (128, 32))
    b2_d = din("b2", (128, 8))
    cq_d = din("cq", (128, 6, 4))
    ck_d = din("ck", (128, 6, 4))
    cv_d = din("cv", (128, 12, 4))
    alog_d = din("alog", (1, NH))
    dtbb_d = din("dtbb", (1, 24))
    gnorm_d = din("gnorm", (1, DV))
    identf_d = din("identf", (128, 128))
    identb_d = din("identb", (128, 128), BF16)
    e64_d = din("e64", (128, 128), BF16)
    utones_d = din("utones", (128, 128))
    masks4_d = din("masks4", (128, 512), BF16)
    i4_d = din("i4", (128, 512), BF16)
    onesrow_d = din("onesrow", (1, 128))
    negonesrow_d = din("negonesrow", (1, 128))

    y_d = nc.dram_tensor("y", [T, D], F32, kind="ExternalOutput").ap()

    w1tern_d = nc.dram_tensor("w1tern", [D, FF], BF16, kind="Internal").ap()
    w2tern_d = nc.dram_tensor("w2tern", [FF, D], BF16, kind="Internal").ap()
    xnfm_d = nc.dram_tensor("xnfm", [NTB, 128, 8, TB], BF16, kind="Internal").ap()
    sg_d = nc.dram_tensor("sgd", [NTB, 128, 8, TB], BF16, kind="Internal").ap()
    crows_d = nc.dram_tensor("crowsd", [NCH, NH, 128], F32, kind="Internal").ap()
    cfm_d = nc.dram_tensor("cfmd", [NCH, NH, 128], F32, kind="Internal").ap()

    p = Prog(nc)

    pctx = []

    def SBP(name, shape, dt=F32):
        cm = nc.sbuf_tensor(name, list(shape), dt)
        pctx.append(cm)
        return cm.__enter__()

    identf = SBP("identf_sb", (128, 128))
    identb = SBP("identb_sb", (128, 128), BF16)
    utones = SBP("utones_sb", (128, 128))
    onesrow = SBP("onesrow_sb", (1, 128))
    negonesrow = SBP("negonesrow_sb", (1, 128))
    c_res = SBP("c_res", (128, NCH, NH))
    b_res = SBP("b_res", (128, NCH, NH))
    be_res = SBP("be_res", (128, NCH, NH))
    epsln = SBP("epsln", (128, 1))
    eps12 = SBP("eps12", (128, 1))
    onecol = SBP("onecol", (128, 1))

    p.dma("SQ", lambda e: e.dma_start(identf[:], identf_d), writes=["identf"])
    p.dma("SQ", lambda e: e.dma_start(identb[:], identb_d), writes=["identb"])
    p.dma("SQ", lambda e: e.dma_start(utones[:], utones_d), writes=["utones"])
    p.dma("SQ", lambda e: e.dma_start(onesrow[:], onesrow_d), writes=["onesrow"])
    p.dma("SQ", lambda e: e.dma_start(negonesrow[:], negonesrow_d), writes=["negonesrow"])
    p.op("POOL", lambda e: e.memset(epsln[:], 1e-5), writes=["epsln"])
    p.op("POOL", lambda e: e.memset(eps12[:], 1e-12), writes=["eps12"])
    p.op("POOL", lambda e: e.memset(onecol[:], 1.0), writes=["onecol"])

    actx = []

    def SBA(name, shape, dt=F32):
        cm = nc.sbuf_tensor(name, list(shape), dt)
        actx.append(cm)
        return cm.__enter__()

    xs = [SBA(f"xs{i}", (128, D)) for i in range(2)]
    sqb = [SBA(f"sqb{i}", (128, D)) for i in range(2)]
    xnt = [SBA(f"xnt{i}", (128, D)) for i in range(2)]
    stat = SBA("stat", (128, 8))
    xnhi = SBA("xnhi", (128, 8, TA), BF16)
    xnf32 = [SBA(f"xnf32{i}", (128, 8, 128)) for i in range(2)]
    wgkb_s = SBA("wgkb_s", (128, 8, 24))
    dtbb_s = SBA("dtbb_s", (128, 24))
    negea_s = SBA("negea_s", (128, NH))
    zb = [SBA(f"zb{i}", (128, 24)) for i in range(2)]
    spb = SBA("spb", (128, 36))
    cfm_s = SBA("cfm_s", (NH, 128))
    w1p = [SBA(f"w1p{i}", (128, 8, 512), BF16) for i in range(2)]
    w2full = SBA("w2full", (128, 32, D), BF16)
    hbuf = SBA("hbuf", (128, 32, 512), BF16)
    sgo = SBA("sgo", (128, 8, 512), BF16)
    b1s = SBA("b1s_sb", (128, 32))
    b2s = SBA("b2s_sb", (128, 8))
    wtf = [SBA(f"wtf{i}", (128, 1024)) for i in range(2)]
    asum = SBA("asum", (128, 40))
    thrc = SBA("thrc", (128, 2))
    terno = SBA("terno", (128, 1024), BF16)
    ternsc = SBA("ternsc", (128, 1024))

    psA, pfA = [], []
    for i in range(8):
        cm = nc.psum_tensor(f"pA{i}", [128, 512], F32)
        psA.append(cm)
        pfA.append(cm.__enter__())

    def ternarize(src_d, dst_d, tagp):
        nrows, ncols = src_d.shape
        colw = min(1024, ncols)
        pieces = [(r, cc) for r in range(nrows // 128) for cc in range(ncols // colw)]
        for i, (r, cc) in enumerate(pieces):
            buf = wtf[i % 2]
            kb = f"{tagp}wtf{i % 2}"
            p.dma("SQ", lambda e, r=r, cc=cc, buf=buf: e.dma_start(
                buf[:, 0:colw], src_d[r * 128:(r + 1) * 128, cc * colw:(cc + 1) * colw]),
                writes=[kb], tag="tern_ld")
            p.op("DVE", lambda e, i=i, buf=buf: e.tensor_reduce(
                asum[:, i % 32:i % 32 + 1], buf[:, 0:colw], axis=AX, op=ALU.add,
                apply_absolute_value=True) if i < 32 else e.scalar_tensor_tensor(
                asum[:, i % 32:i % 32 + 1], buf[:, 0:colw], 0.0, asum[:, i % 32:i % 32 + 1],
                ALU.bypass, ALU.add),
                reads=[kb, "asum"], writes=["asum"], tag="tern_red")
        npieces = len(pieces)
        assert npieces <= 32
        p.op("DVE", lambda e: e.reduce_sum(asum[:, 32:33], asum[:, 0:npieces], axis=AX),
             reads=["asum"], writes=["asum32"])
        p.op("PE", lambda e: e.matmul(pfA[7][0:1, 0:1], asum[:, 32:33], onecol[:],
                                      start=True, stop=True),
             reads=["asum32", "onecol"], writes=["pA7"], tag="tern_tot")
        nelem = float(nrows * ncols)
        p.op("DVE", lambda e: e.tensor_scalar(asum[0:1, 34:35], pfA[7][0:1, 0:1],
                                              1.0 / nelem, 1e-8, ALU.mult, ALU.max),
             reads=["pA7"], writes=["thr0"])
        p.op("DVE", lambda e: e.tensor_scalar(asum[0:1, 35:36], asum[0:1, 34:35],
                                              0.5, None, ALU.mult),
             reads=["thr0"], writes=["thr1"])
        p.op("PE", lambda e: e.matmul(pfA[7][:, 0:1], onesrow[:], asum[0:1, 35:36],
                                      start=True, stop=True),
             reads=["thr1", "onesrow"], writes=["pA7"], tag="tern_bc")
        p.op("DVE", lambda e: e.tensor_copy(thrc[:, 0:1], pfA[7][:, 0:1]),
             reads=["pA7"], writes=["thrc"])
        p.op("DVE", lambda e: e.tensor_scalar(thrc[:, 1:2], thrc[:, 0:1], -1.0, None, ALU.mult),
             reads=["thrc"], writes=["thrc2"])
        for i, (r, cc) in enumerate(pieces):
            buf = wtf[i % 2]
            kb = f"{tagp}wtf{i % 2}"
            p.dma("SQ", lambda e, r=r, cc=cc, buf=buf: e.dma_start(
                buf[:, 0:colw], src_d[r * 128:(r + 1) * 128, cc * colw:(cc + 1) * colw]),
                writes=[kb], tag="tern_ld2")
            p.op("DVE", lambda e, buf=buf: e.tensor_single_scalar(
                ternsc[:, 0:colw], buf[:, 0:colw], thrc[:, 1:2], ALU.is_lt),
                reads=[kb, "thrc2"], writes=["ternsc"], tag="tern_lt")
            p.op("DVE", lambda e, buf=buf: e.scalar_tensor_tensor(
                terno[:, 0:colw], buf[:, 0:colw], thrc[:, 0:1], ternsc[:, 0:colw],
                ALU.is_gt, ALU.subtract),
                reads=[kb, "thrc", "ternsc"], writes=["terno"], tag="tern_gt")
            p.dma("SQ", lambda e, r=r, cc=cc: e.dma_start(
                dst_d[r * 128:(r + 1) * 128, cc * colw:(cc + 1) * colw], terno[:, 0:colw]),
                reads=["terno"], writes=[f"{tagp}tern_done"], tag="tern_st")

    ternarize(w1t_d, w1tern_d, "a")
    ternarize(w2t_d, w2tern_d, "b")

    p.dma("GQ", lambda e: e.dma_start(wgkb_s[:], wgkb_d.rearrange("(c p) q -> p c q", p=128)),
          writes=["wgkb_s"])
    p.dma("SQ", lambda e: e.dma_start(b1s[:], b1_d), writes=["b1s"])
    p.dma("SQ", lambda e: e.dma_start(b2s[:], b2_d), writes=["b2s"])
    p.dma("SQ", lambda e: e.dma_start(spb[0:1, 0:24], dtbb_d), writes=["spb0", "spb1"])
    p.dma("SQ", lambda e: e.dma_start(spb[0:1, 24:36], alog_d), writes=["spb2"])
    p.op("ACT", lambda e: e.activation(spb[0:1, 24:36], spb[0:1, 24:36], AF.Exp),
         reads=["spb2"], writes=["spb2"], tag="ExpA")
    p.op("DVE", lambda e: e.tensor_scalar(spb[0:1, 24:36], spb[0:1, 24:36], -1.0, None, ALU.mult),
         reads=["spb2"], writes=["spb2"])
    p.op("PE", lambda e: e.matmul(pfA[6][:, 0:24], onesrow[:], spb[0:1, 0:24],
                                  start=True, stop=True),
         reads=["spb0", "spb1", "onesrow"], writes=["pA6"])
    p.op("DVE", lambda e: e.tensor_copy(dtbb_s[:], pfA[6][:, 0:24]),
         reads=["pA6"], writes=["dtbb_s"])
    p.op("PE", lambda e: e.matmul(pfA[6][:, 32:44], onesrow[:], spb[0:1, 24:36],
                                  start=True, stop=True),
         reads=["spb2", "onesrow"], writes=["pA6"])
    p.op("DVE", lambda e: e.tensor_copy(negea_s[:], pfA[6][:, 32:44]),
         reads=["pA6"], writes=["negea_s"])

    def emit_ln_strip(a, s):
        gs = a * 8 + s
        xb, sq, xn, xf = xs[gs % 2], sqb[gs % 2], xnt[gs % 2], xnf32[gs % 2]
        kx, ksq, kxn, kxf = (f"xs{gs % 2}", f"sq{gs % 2}", f"xnt{gs % 2}", f"xnf{gs % 2}")
        p.dma("SQ", lambda e: e.dma_start(xb[:], x_d[gs * 128:(gs + 1) * 128, :]),
              writes=[kx], tag="ldx")
        p.op("DVE", lambda e: e.tensor_mul(sq[:], xb[:], xb[:]), reads=[kx], writes=[ksq])
        p.op("DVE", lambda e: e.reduce_sum(stat[:, 0:1], xb[:], axis=AX),
             reads=[kx], writes=["stat0"])
        p.op("DVE", lambda e: e.reduce_sum(stat[:, 1:2], sq[:], axis=AX),
             reads=[ksq], writes=["stat1"])
        p.op("DVE", lambda e: e.tensor_scalar(stat[:, 2:3], stat[:, 0:1], 1.0 / D, None, ALU.mult),
             reads=["stat0"], writes=["stat2"])
        p.op("DVE", lambda e: e.tensor_scalar(stat[:, 3:4], stat[:, 1:2], 1.0 / D, None, ALU.mult),
             reads=["stat1"], writes=["stat3"])
        p.op("DVE", lambda e: e.scalar_tensor_tensor(stat[:, 4:5], stat[:, 2:3], stat[:, 2:3],
                                                     stat[:, 3:4], ALU.mult, ALU.subtract),
             reads=["stat2", "stat3"], writes=["stat4"])
        p.op("DVE", lambda e: e.tensor_scalar(stat[:, 5:6], stat[:, 4:5], -1.0, None, ALU.mult),
             reads=["stat4"], writes=["stat5"])
        p.op("ACT", lambda e: e.activation(stat[:, 6:7], stat[:, 5:6], AF.Ln, bias=epsln[:]),
             reads=["stat5", "epsln"], writes=["stat6"], tag="LnA")
        p.op("ACT", lambda e: e.activation(stat[:, 7:8], stat[:, 6:7], AF.Exp, scale=-0.5),
             reads=["stat6"], writes=["stat7"], tag="ExpA")
        p.op("DVE", lambda e: e.tensor_scalar(xn[:], xb[:], stat[:, 2:3], stat[:, 7:8],
                                              ALU.subtract, ALU.mult),
             reads=[kx, "stat2", "stat7"], writes=[kxn], tag="ln")
        for c in range(8):
            bank = c % 4
            p.op("PE", lambda e, c=c, bank=bank: e.transpose(
                pfA[bank][:, 0:128], xn[:, c * 128:(c + 1) * 128], identf[:]),
                reads=[kxn, "identf"], writes=[f"pA{bank}"], tag="tpA")
            p.op("DVE", lambda e, c=c, bank=bank: e.tensor_copy(
                xnhi[:, c, s * 128:(s + 1) * 128], pfA[bank][:, 0:128]),
                reads=[f"pA{bank}"], writes=["xnhi"], tag="hiev")
            p.op("DVE", lambda e, c=c, bank=bank: e.tensor_copy(
                xf[:, c, :], pfA[bank][:, 0:128]),
                reads=[f"pA{bank}"], writes=[kxf], tag="f32ev")
        for c in range(8):
            p.op("PE", lambda e, c=c: e.matmul(pfA[2][:, 0:24], xf[:, c, :], wgkb_s[:, c, :],
                                               start=(c == 0), stop=(c == 7)),
                 reads=[kxf, "wgkb_s"], writes=["pA2"], tag="gkmm")
        z = zb[gs % 2]
        kz = f"zb{gs % 2}"
        p.op("DVE", lambda e: e.tensor_add(z[:], pfA[2][:, 0:24], dtbb_s[:]),
             reads=["pA2", "dtbb_s"], writes=[kz])
        p.op("DVE", lambda e: e.tensor_scalar(spb[:, 0:12], z[:, 0:12], -1.0, None, ALU.mult),
             reads=[kz], writes=["spb0"])
        p.op("DVE", lambda e: e.tensor_max(spb[:, 0:12], spb[:, 0:12], z[:, 0:12]),
             reads=["spb0", kz], writes=["spb0"])
        p.op("ACT", lambda e: e.activation(spb[:, 12:24], spb[:, 0:12], AF.Exp, scale=-1.0),
             reads=["spb0"], writes=["spb1"], tag="ExpA")
        p.op("ACT", lambda e: e.activation(spb[:, 0:12], spb[:, 12:24], AF.Ln, bias=onecol[:]),
             reads=["spb1", "onecol"], writes=["spb0"], tag="LnA")
        p.op("DVE", lambda e: e.tensor_scalar(spb[:, 12:24], z[:, 0:12], 0.0, None, ALU.max),
             reads=[kz], writes=["spb1"])
        p.op("DVE", lambda e: e.tensor_add(spb[:, 24:36], spb[:, 0:12], spb[:, 12:24]),
             reads=["spb0", "spb1"], writes=["spb2"])
        p.op("DVE", lambda e: e.tensor_mul(spb[:, 0:12], spb[:, 24:36], negea_s[:]),
             reads=["spb2", "negea_s"], writes=["spb0"])
        p.op("PE", lambda e: e.matmul(pfA[3][:, 0:12], utones[:], spb[:, 0:12],
                                      start=True, stop=True),
             reads=["spb0", "utones"], writes=["pA3"], tag="cums")
        p.op("DVE", lambda e, gs=gs: e.tensor_copy(c_res[:, gs, :], pfA[3][:, 0:12]),
             reads=["pA3"], writes=["c_res"])
        p.op("ACT", lambda e, gs=gs: e.activation(b_res[:, gs, :], pfA[3][:, 0:12], AF.Exp),
             reads=["pA3"], writes=["b_res"], tag="ExpA")
        p.op("ACT", lambda e: e.activation(spb[:, 12:24], z[:, 12:24], AF.Exp, scale=-1.0),
             reads=[kz], writes=["spb1"], tag="ExpA")
        p.op("DVE", lambda e: e.tensor_scalar(spb[:, 24:36], spb[:, 12:24], 1.0, None, ALU.add),
             reads=["spb1"], writes=["spb2"])
        p.op("DVE", lambda e, gs=gs: e.reciprocal(be_res[:, gs, :], spb[:, 24:36]),
             reads=["spb2"], writes=["be_res"])
        p.op("PE", lambda e, gs=gs: e.transpose(pfA[4][0:12, 0:128], c_res[:, gs, :], identf[:]),
             reads=["c_res", "identf"], writes=["pA4"], tag="ctp")
        p.op("DVE", lambda e: e.tensor_copy(cfm_s[:], pfA[4][0:12, 0:128]),
             reads=["pA4"], writes=["cfm_s"])
        p.dma("SQ", lambda e, gs=gs: e.dma_start(cfm_d[gs], cfm_s[:]),
              reads=["cfm_s"], writes=["cfm_d"], tag="cst")
        p.dma("SQ", lambda e, gs=gs: e.dma_start(
            crows_d[gs].rearrange("h t -> (h t)").unsqueeze(0),
            cfm_d[gs].rearrange("h t -> (h t)").unsqueeze(0)),
            reads=["cfm_d"], writes=["crows_done"], tag="cro")

    def emit_ffn(a):
        p.dma("SQ", lambda e: e.dma_start(
            w2full[:], w2tern_d.rearrange("(c p) q -> p c q", p=128)),
            reads=["btern_done"], writes=["w2full"], tag="w2ld")
        for sub in range(2):
            tk = slice(sub * 512, (sub + 1) * 512)
            for piece in range(8):
                buf = w1p[piece % 2]
                kb = f"w1p{piece % 2}"
                p.dma("SQ", lambda e, piece=piece, buf=buf: e.dma_start(
                    buf[:], w1tern_d.rearrange("(c p) f -> p c f", p=128)[:, :, piece * 512:(piece + 1) * 512]),
                    reads=["atern_done"], writes=[kb], tag="w1ld")
                for lf in range(4):
                    ffc = piece * 4 + lf
                    bank = ffc % 2
                    for c in range(8):
                        p.op("PE", lambda e, c=c, lf=lf, bank=bank, buf=buf, tk=tk: e.matmul(
                            pfA[bank][:, 0:512], buf[:, c, lf * 128:(lf + 1) * 128],
                            xnhi[:, c, tk], start=(c == 0), stop=(c == 7)),
                            reads=[kb, "xnhi"], writes=[f"pA{bank}"], tag="w1mm")
                    p.op("ACT", lambda e, ffc=ffc, bank=bank: e.activation(
                        hbuf[:, ffc, :], pfA[bank][:, 0:512], AF.Gelu, bias=b1s[:, ffc:ffc + 1]),
                        reads=[f"pA{bank}", "b1s"], writes=["hbuf"], tag="gelu")
            for e_ in range(8):
                for ffc in range(32):
                    p.op("PE", lambda e, e_=e_, ffc=ffc: e.matmul(
                        pfA[e_][:, 0:512], w2full[:, ffc, e_ * 128:(e_ + 1) * 128],
                        hbuf[:, ffc, :], start=(ffc == 0), stop=(ffc == 31)),
                        reads=["w2full", "hbuf"], writes=[f"pA{e_}"], tag="w2mm")
            for e_ in range(8):
                p.op("ACT", lambda e, e_=e_: e.activation(
                    sgo[:, e_, :], pfA[e_][:, 0:512], AF.Silu, bias=b2s[:, e_:e_ + 1]),
                    reads=[f"pA{e_}", "b2s"], writes=["sgo"], tag="ffsilu")
            tb_idx = a * 2 + sub
            p.dma("SQ", lambda e, tb_idx=tb_idx: e.dma_start(sg_d[tb_idx], sgo[:]),
                  reads=["sgo"], writes=["sg_done"], tag="sgst")

    for a in range(NTA):
        for s in range(8):
            emit_ln_strip(a, s)
        emit_ffn(a)
        for half in range(2):
            tb_idx = a * 2 + half
            p.dma("SQ", lambda e, tb_idx=tb_idx, half=half: e.dma_start(
                xnfm_d[tb_idx], xnhi[:, :, half * 512:(half + 1) * 512]),
                reads=["xnhi"], writes=["xn_done"], tag="xnst")

    p.global_barrier()
    for cm in reversed(psA):
        cm.__exit__(None, None, None)
    for cm in reversed(actx):
        cm.__exit__(None, None, None)

    bctx = []

    def SBB(name, shape, dt=F32):
        cm = nc.sbuf_tensor(name, list(shape), dt)
        bctx.append(cm)
        return cm.__enter__()

    wvg = [SBB(f"wvg{i}", (128, 8, 512), BF16) for i in range(2)]
    wo_s = [SBB(f"wo_s{i}", (128, 3, D), BF16) for i in range(2)]
    xnb = [SBB(f"xnb{i}", (128, 8, TB), BF16) for i in range(2)]
    sgb = SBB("sgb", (128, 8, TB), BF16)
    qs_ = SBB("qs", (128, 6, TB), BF16)
    ks_ = SBB("ks", (128, 6, TB), BF16)
    vs_ = SBB("vs", (128, 12, TB), BF16)
    gt_ = SBB("gt", (128, 4, VD), BF16)
    ext = [SBB(f"ext{i}", (128, 516), BF16) for i in range(3)]
    cvo = [SBB(f"cvo{i}", (128, 512), BF16) for i in range(2)]
    stash = SBB("stash", (128, 24, 4), BF16)
    sqv = SBB("sqv", (128, 512), BF16)
    rno = [SBB(f"rno{i}", (128, 512)) for i in range(2)]
    cqs = SBB("cqs_sb", (128, 6, 4))
    cks = SBB("cks_sb", (128, 6, 4))
    cvs = SBB("cvs_sb", (128, 12, 4))
    masks4 = SBB("masks4_sb", (128, 512), BF16)
    i4 = SBB("i4_sb", (128, 512), BF16)
    e64 = SBB("e64_sb", (128, 128), BF16)
    gnw4 = SBB("gnw4", (128, 512))
    crows = [SBB(f"crows{i}", (1, 1536)) for i in range(2)]
    bcrow = SBB("bcrow", (1, NH))
    bccol = SBB("bccol", (128, NH))
    S_ = SBB("S", (128, 6, DV))
    Sbf = SBB("Sbf", (128, 6, DV), BF16)
    e1lg = SBB("e1lg", (128, 512))
    e1s = [SBB(f"e1s{i}", (128, 512), BF16) for i in range(2)]
    e1i = [SBB(f"e1i{i}", (128, 512), BF16) for i in range(2)]
    sB0 = [SBB(f"sB0{i}", (128, 512), BF16) for i in range(2)]
    sA0 = [SBB(f"sA0{i}", (128, 512), BF16) for i in range(2)]
    sU = [SBB(f"sU{i}", (128, 512), BF16) for i in range(3)]
    sWd = [SBB(f"sWd{i}", (128, 512), BF16) for i in range(2)]
    skbb = [[SBB(f"skbb{i}_{hl}", (128, 256), BF16) for hl in range(2)] for i in range(2)]
    sWx = [[SBB(f"sWx{s_}_{hl}", (128, 256), BF16) for hl in range(2)] for s_ in range(2)]
    sWkF = [[SBB(f"sWkF{i}_{hl}", (128, 256), BF16) for hl in range(2)] for i in range(2)]
    wtf32 = [SBB(f"wtf32{i}", (128, 256)) for i in range(2)]
    sAx = [[SBB(f"sAx{s_}_{i}", (128, 512), BF16) for i in range(2)] for s_ in range(2)]
    sBx = [[SBB(f"sBx{s_}_{i}", (128, 512), BF16) for i in range(2)] for s_ in range(2)]
    sBt = [SBB(f"sBt{s_}", (128, 512), BF16) for s_ in range(2)]
    sDx = [[SBB(f"sDx{s_}_{hl}", (128, 512), BF16) for hl in range(2)] for s_ in range(2)]
    khat = [SBB(f"khat{i}", (128, 256), BF16) for i in range(2)]
    oS = [SBB(f"oS{i}", (128, 512)) for i in range(2)]
    o2S = [SBB(f"o2S{i}", (128, 512)) for i in range(2)]
    rsts = SBB("rsts", (128, 16))
    rcols = SBB("rcols", (128, 8))
    ogt = [SBB(f"ogt{i}", (128, 512), BF16) for i in range(2)]
    ofm = SBB("ofm", (128, 12, TB), BF16)
    xre = SBB("xre", (128, 4, D))

    psB, pfB = [], []
    for i in range(6):
        cm = nc.psum_tensor(f"pB{i}", [128, 512], F32)
        psB.append(cm)
        pfB.append(cm.__enter__())
    pb0cm = nc.psum_tensor("pb0", [128, 1024], BF16)
    psB.append(pb0cm)
    pb0 = pb0cm.__enter__()
    pb1cm = nc.psum_tensor("pb1", [128, 1024], BF16)
    psB.append(pb1cm)
    pb1 = pb1cm.__enter__()

    p.dma("SQ", lambda e: e.dma_start(masks4[:], masks4_d), writes=["masks4"])
    p.dma("SQ", lambda e: e.dma_start(i4[:], i4_d), writes=["i4"])
    p.dma("SQ", lambda e: e.dma_start(e64[:], e64_d), writes=["e64"])
    p.dma("SQ", lambda e: e.dma_start(cqs[:], cq_d), writes=["cqs"])
    p.dma("SQ", lambda e: e.dma_start(cks[:], ck_d), writes=["cks"])
    p.dma("SQ", lambda e: e.dma_start(cvs[:], cv_d), writes=["cvs"])
    p.dma("SQ", lambda e: e.dma_start(gnw4[0:1, 0:DV], gnorm_d), writes=["gnwrow"])
    for q in range(4):
        p.op("PE", lambda e, q=q: e.matmul(pfB[0][:, q * 128:(q + 1) * 128], onesrow[:],
                                           gnw4[0:1, 0:DV], start=(q == 0), stop=(q == 3)),
             reads=["gnwrow", "onesrow"], writes=["pB0"])
    p.op("DVE", lambda e: e.tensor_copy(gnw4[:], pfB[0][:, 0:512]),
         reads=["pB0"], writes=["gnw4"])
    p.op("POOL", lambda e: e.memset(S_[:], 0.0), writes=["S"])
    p.op("POOL", lambda e: e.memset(Sbf[:], 0.0), writes=["Sbf"])
    p.op("POOL", lambda e: e.memset(stash[:], 0.0), writes=["stash"])
    p.dma("SQ", lambda e: e.dma_start(
        crows[0][:], crows_d[0].rearrange("h t -> (h t)").unsqueeze(0)),
        reads=["crows_done"], writes=["crows0"], tag="crold0")

    def emit_frontend(t):
        sl = t % 2
        kxn = f"xnb{sl}"
        p.dma("SQ", lambda e, t=t, sl=sl: e.dma_start(xnb[sl][:], xnfm_d[t]),
              reads=["xn_done"], writes=[kxn], tag="xnld")
        pool_ctr = [0]
        for name, nch, cw, dst, kdst, wsrc, cpp in [
                ("q", 6, cqs, qs_, "qs", wq_d, 3), ("k", 6, cks, ks_, "ks", wk_d, 3),
                ("v", 12, cvs, vs_, "vs", wv_d, 4)]:
            for j in range(nch):
                bank = j % 2
                kbank = f"pB{bank}"
                if j % cpp == 0:
                    pi = pool_ctr[0]
                    pool_ctr[0] += 1
                    buf_ = wvg[pi % 2]
                    w0 = j * 128
                    wwid = cpp * 128
                    p.dma("GQ", lambda e, buf_=buf_, w0=w0, wwid=wwid, wsrc=wsrc: e.dma_start(
                        buf_[:, :, 0:wwid],
                        wsrc.rearrange("(c p) q -> p c q", p=128)[:, :, w0:w0 + wwid]),
                        writes=[f"wvg{pi % 2}"], tag="wpld")
                pi = pool_ctr[0] - 1
                buf = wvg[pi % 2]
                wkey = f"wvg{pi % 2}"
                wap = lambda c, j=j, buf=buf, cpp=cpp: buf[:, c, (j % cpp) * 128:(j % cpp + 1) * 128]
                for c in range(8):
                    p.op("PE", lambda e, c=c, wap=wap, bank=bank, sl=sl: e.matmul(
                        pfB[bank][:, 0:512], wap(c), xnb[sl][:, c, :],
                        start=(c == 0), stop=(c == 7)),
                        reads=[wkey, kxn], writes=[kbank], tag="proj")
                eb = ext[j % 3]
                keb = f"ext{j % 3}"
                sidx = {"q": 0, "k": 6, "v": 12}[name] + j
                p.op("DVE", lambda e, eb=eb, sidx=sidx: e.tensor_copy(
                    eb[:, 0:3], stash[:, sidx, 0:3]),
                    reads=["stash"], writes=[keb], tag="halo")
                p.op("ACT", lambda e, eb=eb, bank=bank: e.copy(eb[:, 3:515], pfB[bank][:, 0:512]),
                     reads=[kbank], writes=[keb], tag="rawev")
                p.op("DVE", lambda e, eb=eb, sidx=sidx: e.tensor_copy(
                    stash[:, sidx, 0:3], eb[:, 512:515]),
                    reads=[keb], writes=["stash"], tag="stashw")
                co = cvo[j % 2]
                kco = f"cvo{j % 2}"
                kcw = "c" + name + "s"
                p.op("DVE", lambda e, eb=eb, cw=cw, j=j, co=co: e.tensor_scalar(
                    co[:], eb[:, 0:512], cw[:, j, 0:1], None, ALU.mult),
                    reads=[keb, kcw], writes=[kco], tag="cv0")
                for tap in range(1, 4):
                    p.op("DVE", lambda e, eb=eb, cw=cw, j=j, co=co, tap=tap: e.scalar_tensor_tensor(
                        co[:], eb[:, tap:tap + 512], cw[:, j, tap:tap + 1], co[:],
                        ALU.mult, ALU.add),
                        reads=[keb, kcw, kco], writes=[kco], tag="cvt")
                p.op("ACT", lambda e, co=co, dst=dst, j=j: e.activation(
                    dst[:, j, :], co[:], AF.Silu),
                    reads=[kco], writes=[kdst], tag="silu")
        for pi in range(3):
            buf = wvg[pi % 2]
            kb = f"wvg{pi % 2}"
            p.dma("GQ", lambda e, pi=pi, buf=buf: e.dma_start(
                buf[:], wg_d.rearrange("(c p) q -> p c q", p=128)[:, :, pi * 512:(pi + 1) * 512]),
                writes=[kb], tag="wgld")
            for tc in range(4):
                bank = (pi * 4 + tc) % 2 + 2
                kbank = f"pB{bank}"
                for c in range(8):
                    p.op("PE", lambda e, c=c, tc=tc, buf=buf, bank=bank, sl=sl: e.matmul(
                        pfB[bank][:, 0:512], xnb[sl][:, c, tc * 128:(tc + 1) * 128],
                        buf[:, c, :], start=(c == 0), stop=(c == 7)),
                        reads=[kb, kxn], writes=[kbank], tag="gproj")
                p.op("ACT", lambda e, tc=tc, pi=pi, bank=bank: e.activation(
                    gt_[:, tc, pi * 512:(pi + 1) * 512], pfB[bank][:, 0:512], AF.Silu),
                    reads=[kbank], writes=["gt"], tag="gsilu")
        for name, dst, kdst in [("q", qs_, "qs"), ("k", ks_, "ks")]:
            for j in range(6):
                bank = 4 + (j % 2)
                kbank = f"pB{bank}"
                rb, krb = rno[j % 2], f"rno{j % 2}"
                p.op("DVE", lambda e, dst=dst, j=j: e.tensor_mul(sqv[:], dst[:, j, :], dst[:, j, :]),
                     reads=[kdst], writes=["sqv"], tag="l2sq")
                p.op("PE", lambda e, bank=bank: e.matmul(pfB[bank][:, 0:512], e64[:], sqv[:],
                                                         start=True, stop=True),
                     reads=["sqv", "e64"], writes=[kbank], tag="l2mm")
                p.op("ACT", lambda e, bank=bank, rb=rb: e.activation(
                    rb[:], pfB[bank][:, 0:512], AF.Ln, bias=eps12[:]),
                    reads=[kbank, "eps12"], writes=[krb], tag="l2ln")
                p.op("ACT", lambda e, rb=rb: e.activation(rb[:], rb[:], AF.Exp, scale=-0.5),
                     reads=[krb], writes=[krb], tag="l2ex")
                p.op("DVE", lambda e, dst=dst, j=j, rb=rb: e.tensor_mul(
                    dst[:, j, :], dst[:, j, :], rb[:]),
                    reads=[kdst, krb], writes=[kdst], tag="l2mul")

    def Kf(h, c):
        return ks_[(h % 2) * 64:(h % 2) * 64 + 64, h // 2, c * 128:(c + 1) * 128]

    def Qf(h, c):
        return qs_[(h % 2) * 64:(h % 2) * 64 + 64, h // 2, c * 128:(c + 1) * 128]

    def Sm(h):
        return S_[(h % 2) * 64:(h % 2) * 64 + 64, h // 2, :]

    def Sb(h):
        return Sbf[(h % 2) * 64:(h % 2) * 64 + 64, h // 2, :]

    def idb64(h):
        o = (h % 2) * 64
        return identb[o:o + 64, o:o + 64]

    def group_steps(t, c, g):
        gc = t * 4 + c
        gi = gc * 3 + g
        sl = gi % 2
        sl3 = gi % 3
        P = [2, 4, 0][sl3]
        X = P + 1
        kP = f"pB{P}"
        kX = f"pB{X}"
        pbx = pb0 if sl == 0 else pb1
        kpbx = f"pb{sl}"
        heads = [4 * g, 4 * g + 2, 4 * g + 1, 4 * g + 3]
        dh = dbg and (t == 0) and (c == 1) and (g == 2)
        dh0 = dbg and (t == 0) and (c == 0) and (g == 2)
        pro = []

        def s_crows():
            if g == 2 and gc + 1 < NCH:
                p.dma("SQ", lambda e, gc=gc: e.dma_start(
                    crows[(gc + 1) % 2][:], crows_d[gc + 1].rearrange("h t -> (h t)").unsqueeze(0)),
                    reads=["crows_done"], writes=[f"crows{(gc + 1) % 2}"], tag="crold")
        pro.append(s_crows)

        def s_ccol():
            for q, h in enumerate(heads):
                p.op("PE", lambda e, q=q, h=h: e.matmul(
                    pfB[P][:, q * 128:(q + 1) * 128], onesrow[:], crows[gc % 2][0:1, h * 128:(h + 1) * 128],
                    start=(q == 0), stop=False),
                    reads=[f"crows{gc % 2}", "onesrow"], writes=[kP], tag="ccol", pe_rows=(0, 1))
                p.op("PE", lambda e, q=q, h=h: e.matmul(
                    pfB[P][:, q * 128:(q + 1) * 128], crows[gc % 2][0:1, h * 128:(h + 1) * 128],
                    negonesrow[:], start=False, stop=(q == 3)),
                    reads=[f"crows{gc % 2}", "negonesrow"], writes=[kP], tag="ccol2", pe_rows=(0, 1))
            p.op("DVE", lambda e: e.tensor_add(e1lg[:], pfB[P][:, 0:512], masks4[:]),
                 reads=[kP, "masks4"], writes=["e1lg"], tag="mask")
            p.op("ACT", lambda e: e.activation(e1s[sl][:], e1lg[:], AF.Exp),
                 reads=["e1lg"], writes=[f"e1s{sl}"], tag="e1exp")
            p.op("POOL", lambda e: e.tensor_add(e1i[sl][:], e1s[sl][:], i4[:]),
                 reads=[f"e1s{sl}", "i4"], writes=[f"e1i{sl}"], tag="e1i")
        pro.append(s_ccol)

        def s_graw():
            for q, h in enumerate(heads):
                p.op("PE", lambda e, q=q, h=h: e.matmul(
                    pfB[P][:, q * 128:(q + 1) * 128], Kf(h, c), Kf(h, c),
                    start=True, stop=True),
                    reads=["ks"], writes=[kP], tag="graw",
                    pe_rows=((h % 2) * 64, (h % 2) * 64 + 64))
            p.op("DVE", lambda e: e.tensor_mul(sB0[sl][:], pfB[P][:, 0:512], e1s[sl][:]),
                 reads=[kP, f"e1s{sl}"], writes=[f"sB0{sl}"], tag="b0a")
            for q, h in enumerate(heads):
                p.op("DVE", lambda e, q=q, h=h: e.tensor_scalar(
                    sB0[sl][:, q * 128:(q + 1) * 128], sB0[sl][:, q * 128:(q + 1) * 128],
                    be_res[:, gc, h:h + 1], -1.0, ALU.mult, ALU.mult),
                    reads=[f"sB0{sl}", "be_res"], writes=[f"sB0{sl}"], tag="b0b")
            if dh:
                p.dma("SQ", lambda e: e.dma_start(dbg_b0, sB0[sl][:]), reads=[f"sB0{sl}"])
        pro.append(s_graw)

        def s_wraw():
            for q, h in enumerate(heads):
                p.op("PE", lambda e, q=q, h=h: e.matmul(
                    pfB[P][:, q * 128:(q + 1) * 128], Kf(h, c), Qf(h, c),
                    start=True, stop=True),
                    reads=["ks", "qs"], writes=[kP], tag="wraw",
                    pe_rows=((h % 2) * 64, (h % 2) * 64 + 64))
            p.op("DVE", lambda e: e.tensor_mul(sWd[sl][:], pfB[P][:, 0:512], e1i[sl][:]),
                 reads=[kP, f"e1i{sl}"], writes=[f"sWd{sl}"], tag="wd")
        pro.append(s_wraw)

        def s_vt():
            for q, h in enumerate(heads):
                p.op("PE", lambda e, q=q, h=h: e.transpose(
                    pbx[:, q * 128:(q + 1) * 128], vs_[:, h, c * 128:(c + 1) * 128], identb[:]),
                    reads=["vs", "identb"], writes=[kpbx], tag="vt")
            p.op("ACT", lambda e: e.copy(sU[sl3][:], pbx[:, 0:512]),
                 reads=[kpbx], writes=[f"sU{sl3}"], tag="u")
        pro.append(s_vt)

        def s_ktm():
            for q, h in enumerate(heads):
                p.op("PE", lambda e, q=q, h=h: e.transpose(
                    pbx[:, 512 + q * 64:512 + (q + 1) * 64], Kf(h, c), idb64(h)),
                    reads=["ks", "identb"], writes=[kpbx], tag="ktm",
                    pe_rows=((h % 2) * 64, (h % 2) * 64 + 64))
            p.op("DVE", lambda e: e.tensor_copy(
                rcols[:, sl * 4:sl * 4 + 4],
                e1i[sl][:].rearrange("p (q t) -> p q t", q=4)[:, :, 127:128].squeeze(2)),
                reads=[f"e1i{sl}"], writes=[f"rcols{sl}"], tag="rcol")
            for q, h in enumerate(heads):
                p.op("DVE", lambda e, q=q, h=h: e.tensor_scalar(
                    khat[sl][:, q * 64:(q + 1) * 64], pbx[:, 512 + q * 64:512 + (q + 1) * 64],
                    rcols[:, sl * 4 + q:sl * 4 + q + 1], None, ALU.mult),
                    reads=[kpbx, f"rcols{sl}"], writes=[f"khat{sl}"], tag="khat")
            for q, h in enumerate(heads):
                p.op("DVE", lambda e, q=q, h=h: e.tensor_scalar(
                    wtf32[sl][:, q * 64:(q + 1) * 64], pbx[:, 512 + q * 64:512 + (q + 1) * 64],
                    b_res[:, gc, h:h + 1], None, ALU.mult),
                    reads=[kpbx, "b_res"], writes=[f"wtf32{sl}"], tag="kbb")
            p.op("DVE", lambda e: e.tensor_copy(skbb[sl][0][:], wtf32[sl][:]),
                 reads=[f"wtf32{sl}"], writes=[f"skbb{sl}0"], tag="kbbh")
            p.op("DVE", lambda e: e.scalar_tensor_tensor(
                skbb[sl][1][:], wtf32[sl][:], 1.0, skbb[sl][0][:], ALU.mult, ALU.subtract),
                reads=[f"wtf32{sl}", f"skbb{sl}0"], writes=[f"skbb{sl}1"], tag="kbbl")
            if dh0:
                p.dma("SQ", lambda e: e.dma_start(dbg_khat0, khat[sl][:]), reads=[f"khat{sl}"])
            if dh:
                p.dma("SQ", lambda e: e.dma_start(dbg_kbb[:, 0, :], skbb[sl][0][:]), reads=[f"skbb{sl}0"])
                p.dma("SQ", lambda e: e.dma_start(dbg_kbb[:, 1, :], skbb[sl][1][:]), reads=[f"skbb{sl}1"])
                p.dma("SQ", lambda e: e.dma_start(dbg_khat, khat[sl][:]), reads=[f"khat{sl}"])
        pro.append(s_ktm)

        def s_a0():
            for q in range(4):
                p.op("PE", lambda e, q=q: e.transpose(
                    pbx[:, q * 128:(q + 1) * 128], sB0[sl][:, q * 128:(q + 1) * 128], identb[:]),
                    reads=[f"sB0{sl}", "identb"], writes=[kpbx], tag="a0")
            p.op("ACT", lambda e: e.copy(sA0[sl][:], pbx[:, 0:512]),
                 reads=[kpbx], writes=[f"sA0{sl}"], tag="a0ev")
        pro.append(s_a0)

        body = []
        Acur = (sA0[sl], f"sA0{sl}")
        Bcur = (sB0[sl], f"sB0{sl}")
        Dcur = (sU[sl3], f"sU{sl3}")
        Wcur = (skbb[sl], f"skbb{sl}")

        def mk_level(j, Acur, Bcur, Dcur, Wcur):
            Ain, kAin = Acur
            Bin, kBin = Bcur
            Din, kDin = Dcur
            Win, kWin = Wcur
            Aout, kAout = sAx[sl][j % 2], f"sAx{sl}_{j % 2}"
            Bout, kBout = sBx[sl][j % 2], f"sBx{sl}_{j % 2}"
            Dout, kDout = sDx[sl], f"sDx{sl}_"
            Wout, kWout = sWx[sl], f"sWx{sl}_"

            def step():
                if j == 0:
                    p.op("POOL", lambda e: e.tensor_add(sBt[sl][:], Bin[:], i4[:]),
                         reads=[kBin, "i4"], writes=[f"sBt{sl}"], tag="bt")
                for q in range(4):
                    if j == 0:
                        p.op("PE", lambda e, q=q: e.matmul(
                            pfB[X][:, q * 128:(q + 1) * 128], sBt[sl][:, q * 128:(q + 1) * 128],
                            Din[:, q * 128:(q + 1) * 128], start=True, stop=True),
                            reads=[f"sBt{sl}", kDin], writes=[kX], tag="apl")
                    else:
                        p.op("PE", lambda e, q=q: e.matmul(
                            pfB[X][:, q * 128:(q + 1) * 128], sBt[sl][:, q * 128:(q + 1) * 128],
                            Din[0][:, q * 128:(q + 1) * 128], start=True, stop=False),
                            reads=[f"sBt{sl}", kDin + "0"], writes=[kX], tag="apl")
                        p.op("PE", lambda e, q=q: e.matmul(
                            pfB[X][:, q * 128:(q + 1) * 128], sBt[sl][:, q * 128:(q + 1) * 128],
                            Din[1][:, q * 128:(q + 1) * 128], start=False, stop=True),
                            reads=[f"sBt{sl}", kDin + "1"], writes=[kX], tag="apl2")
                    p.op("PE", lambda e, q=q: e.matmul(
                        pfB[P][:, q * 64:(q + 1) * 64], sBt[sl][:, q * 128:(q + 1) * 128],
                        Win[0][:, q * 64:(q + 1) * 64], start=True, stop=False),
                        reads=[f"sBt{sl}", kWin + "0"], writes=[kP], tag="awk")
                    p.op("PE", lambda e, q=q: e.matmul(
                        pfB[P][:, q * 64:(q + 1) * 64], sBt[sl][:, q * 128:(q + 1) * 128],
                        Win[1][:, q * 64:(q + 1) * 64], start=False, stop=True),
                        reads=[f"sBt{sl}", kWin + "1"], writes=[kP], tag="awk2")
                dst, kdst = Dout, kDout
                p.op("ACT", lambda e, dst=dst: e.copy(dst[0][:], pfB[X][:, 0:512]),
                     reads=[kX], writes=[kdst + "0"], tag="dev")
                p.op("DVE", lambda e, dst=dst: e.scalar_tensor_tensor(
                    dst[1][:], pfB[X][:, 0:512], 1.0, dst[0][:], ALU.mult, ALU.subtract),
                    reads=[kX, kdst + "0"], writes=[kdst + "1"], tag="devl")
                if j < 6:
                    p.op("ACT", lambda e: e.copy(Wout[0][:], pfB[P][:, 0:256]),
                         reads=[kP], writes=[kWout + "0"], tag="wevh")
                    p.op("DVE", lambda e: e.scalar_tensor_tensor(
                        Wout[1][:], pfB[P][:, 0:256], 1.0, Wout[0][:], ALU.mult, ALU.subtract),
                        reads=[kP, kWout + "0"], writes=[kWout + "1"], tag="wevl")
                else:
                    p.op("DVE", lambda e: e.tensor_scalar(
                        Wout[0][:], pfB[P][:, 0:256], -1.0, None, ALU.mult),
                        reads=[kP], writes=[kWout + "0"], tag="wkfh")
                    p.op("DVE", lambda e: e.scalar_tensor_tensor(
                        Wout[1][:], pfB[P][:, 0:256], -1.0, Wout[0][:], ALU.mult, ALU.subtract),
                        reads=[kP, kWout + "0"], writes=[kWout + "1"], tag="wkfl")
                    if dh:
                        p.dma("SQ", lambda e: e.dma_start(dbg_wkt[:, 0, :], Wout[0][:]), reads=[kWout + "0"])
                        p.dma("SQ", lambda e: e.dma_start(dbg_wkt[:, 1, :], Wout[1][:]), reads=[kWout + "1"])
                        p.dma("SQ", lambda e: e.dma_start(dbg_dv[:, 0, :], sDx[sl][0][:]), reads=[f"sDx{sl}_0"])
                        p.dma("SQ", lambda e: e.dma_start(dbg_dv[:, 1, :], sDx[sl][1][:]), reads=[f"sDx{sl}_1"])
                if j < 6:
                    for q in range(4):
                        p.op("PE", lambda e, q=q: e.matmul(
                            pfB[P][:, q * 128:(q + 1) * 128], Bin[:, q * 128:(q + 1) * 128],
                            Ain[:, q * 128:(q + 1) * 128], start=True, stop=True),
                            reads=[kBin, kAin], writes=[kP], tag="sqa")
                    for q in range(4):
                        p.op("PE", lambda e, q=q: e.matmul(
                            pfB[X][:, q * 128:(q + 1) * 128], Ain[:, q * 128:(q + 1) * 128],
                            Bin[:, q * 128:(q + 1) * 128], start=True, stop=True),
                            reads=[kAin, kBin], writes=[kX], tag="sqb")
                    p.op("ACT", lambda e: e.copy(Aout[:], pfB[P][:, 0:512]),
                         reads=[kP], writes=[kAout], tag="aev")
                    p.op("DVE", lambda e: e.tensor_copy(Bout[:], pfB[X][:, 0:512]),
                         reads=[kX], writes=[kBout], tag="bev")
                    p.op("POOL", lambda e: e.tensor_add(sBt[sl][:], Bout[:], i4[:]),
                         reads=[kBout, "i4"], writes=[f"sBt{sl}"], tag="btev")
            return step

        for j in range(7):
            body.append(mk_level(j, Acur, Bcur, Dcur, Wcur))
            if j < 6:
                Acur = (sAx[sl][j % 2], f"sAx{sl}_{j % 2}")
                Bcur = (sBx[sl][j % 2], f"sBx{sl}_{j % 2}")
                Dcur = (sDx[sl], f"sDx{sl}_")
                Wcur = (sWx[sl], f"sWx{sl}_")

        def s_wkfm():
            for hl in range(2):
                for q, h in enumerate(heads):
                    o = (h % 2) * 64
                    p.op("PE", lambda e, q=q, h=h, o=o, hl=hl: e.transpose(
                        pbx[o:o + 64, hl * 512 + q * 128:hl * 512 + (q + 1) * 128],
                        sWx[sl][hl][:, q * 64:(q + 1) * 64], identb[:]),
                        reads=[f"sWx{sl}_{hl}", "identb"], writes=[kpbx], tag="wkfm")
            for hl in range(2):
                p.op("DVE", lambda e, hl=hl: e.tensor_copy(
                    sWkF[sl][hl][0:64, 0:256], pbx[0:64, hl * 512:hl * 512 + 256]),
                    reads=[kpbx], writes=[f"sWkF{sl}{hl}"], tag="wkfev")
                p.op("DVE", lambda e, hl=hl: e.tensor_copy(
                    sWkF[sl][hl][64:128, 0:256], pbx[64:128, hl * 512 + 256:hl * 512 + 512]),
                    reads=[kpbx], writes=[f"sWkF{sl}{hl}"], tag="wkfev2")
            if dh:
                for hl in range(2):
                    p.dma("SQ", lambda e, hl=hl: e.dma_start(dbg_wkf[:, hl, :], sWkF[sl][hl][:]), reads=[f"sWkF{sl}{hl}"])
        body.append(s_wkfm)

        def s_r2a():
            p.op("PE", lambda e: e.matmul(pfB[X][:, 0:512], identb[:], sDx[sl][0][:],
                                          start=True, stop=False),
                 reads=["identb", f"sDx{sl}_0"], writes=[kX], tag="r2i0")
            p.op("PE", lambda e: e.matmul(pfB[X][:, 0:512], identb[:], sDx[sl][1][:],
                                          start=False, stop=False),
                 reads=["identb", f"sDx{sl}_1"], writes=[kX], tag="r2i1")
            for q, h in enumerate(heads):
                o = (h % 2) * 64
                last = (q == 3)
                p.op("PE", lambda e, q=q, h=h, o=o: e.matmul(
                    pfB[X][:, q * 128:(q + 1) * 128], sWkF[sl][0][o:o + 64, (q % 2) * 128:(q % 2 + 1) * 128],
                    Sb(h), start=False, stop=False),
                    reads=[f"sWkF{sl}0", "Sbf"], writes=[kX], tag="wks",
                    pe_rows=(o, o + 64))
                p.op("PE", lambda e, q=q, h=h, o=o, last=last: e.matmul(
                    pfB[X][:, q * 128:(q + 1) * 128], sWkF[sl][1][o:o + 64, (q % 2) * 128:(q % 2 + 1) * 128],
                    Sb(h), start=False, stop=last),
                    reads=[f"sWkF{sl}1", "Sbf"], writes=[kX], tag="wks2",
                    pe_rows=(o, o + 64))
            for q, h in enumerate(heads):
                p.op("DVE", lambda e, q=q, h=h: e.tensor_scalar(
                    sU[sl3][:, q * 128:(q + 1) * 128], pfB[X][:, q * 128:(q + 1) * 128],
                    be_res[:, gc, h:h + 1], None, ALU.mult),
                    reads=[kX, "be_res"], writes=[f"sU{sl3}"], tag="dfc")
            if dh:
                p.dma("SQ", lambda e: e.dma_start(dbg_df, sU[sl3][:]), reads=[f"sU{sl3}"])
                p.dma("SQ", lambda e: e.dma_start(dbg_sbf, Sbf[:]), reads=["Sbf"])
            if dh0:
                p.dma("SQ", lambda e: e.dma_start(dbg_df0, sU[sl3][:]), reads=[f"sU{sl3}"])
        body.append(s_r2a)

        def s_out():
            for q, h in enumerate(heads):
                p.op("PE", lambda e, q=q, h=h: e.matmul(
                    pfB[X][:, q * 128:(q + 1) * 128], Qf(h, c), Sb(h),
                    start=True, stop=True),
                    reads=["qs", "Sbf"], writes=[kX], tag="oint",
                    pe_rows=((h % 2) * 64, (h % 2) * 64 + 64))
            for q in range(4):
                p.op("PE", lambda e, q=q: e.matmul(
                    pfB[P][:, q * 128:(q + 1) * 128], sWd[sl][:, q * 128:(q + 1) * 128],
                    sU[sl3][:, q * 128:(q + 1) * 128], start=True, stop=True),
                    reads=[f"sWd{sl}", f"sU{sl3}"], writes=[kP], tag="ointra")
            for q, h in enumerate(heads):
                p.op("DVE", lambda e, q=q, h=h: e.tensor_scalar(
                    oS[sl][:, q * 128:(q + 1) * 128], pfB[X][:, q * 128:(q + 1) * 128],
                    b_res[:, gc, h:h + 1], None, ALU.mult),
                    reads=[kX, "b_res"], writes=[f"oS{sl}"], tag="oev1")
            p.op("DVE", lambda e: e.tensor_add(oS[sl][:], oS[sl][:], pfB[P][:, 0:512]),
                 reads=[kP, f"oS{sl}"], writes=[f"oS{sl}"], tag="oev2")
        body.append(s_out)

        def s_gate():
            p.op("POOL", lambda e: e.tensor_mul(o2S[sl][:], oS[sl][:], oS[sl][:]),
                 reads=[f"oS{sl}"], writes=[f"o2S{sl}"], tag="gsq")
            p.op("DVE", lambda e: e.reduce_sum(
                rsts[:, sl * 8:sl * 8 + 4],
                o2S[sl][:].rearrange("p (q v) -> p q v", q=4), axis=AX),
                reads=[f"o2S{sl}"], writes=[f"rst{sl}"], tag="gred")
            p.op("ACT", lambda e: e.activation(rsts[:, sl * 8 + 4:sl * 8 + 8],
                                               rsts[:, sl * 8:sl * 8 + 4], AF.Ln,
                                               bias=epsln[:], scale=1.0 / DV),
                 reads=[f"rst{sl}", "epsln"], writes=[f"rstl{sl}"], tag="gln")
            p.op("ACT", lambda e: e.activation(rsts[:, sl * 8 + 4:sl * 8 + 8],
                                               rsts[:, sl * 8 + 4:sl * 8 + 8], AF.Exp, scale=-0.5),
                 reads=[f"rstl{sl}"], writes=[f"rstl{sl}"], tag="gex")
            for q in range(4):
                p.op("DVE", lambda e, q=q: e.tensor_scalar(
                    oS[sl][:, q * 128:(q + 1) * 128], oS[sl][:, q * 128:(q + 1) * 128],
                    rsts[:, sl * 8 + 4 + q:sl * 8 + 5 + q], None, ALU.mult),
                    reads=[f"oS{sl}", f"rstl{sl}"], writes=[f"oS{sl}"], tag="grs")
            p.op("POOL", lambda e: e.tensor_mul(o2S[sl][:], oS[sl][:], gnw4[:]),
                 reads=[f"oS{sl}", "gnw4"], writes=[f"o2S{sl}"], tag="ggnw")
            for q, h in enumerate(heads):
                p.op("POOL", lambda e, q=q, h=h: e.tensor_mul(
                    ogt[sl][:, q * 128:(q + 1) * 128], o2S[sl][:, q * 128:(q + 1) * 128],
                    gt_[:, c, h * 128:(h + 1) * 128]),
                    reads=[f"o2S{sl}", "gt"], writes=[f"ogt{sl}"], tag="gmul")
        body.append(s_gate)

        def s_ogtp():
            for q in range(4):
                p.op("PE", lambda e, q=q: e.transpose(
                    pbx[:, q * 128:(q + 1) * 128], ogt[sl][:, q * 128:(q + 1) * 128],
                    identb[:]),
                    reads=[f"ogt{sl}", "identb"], writes=[kpbx], tag="ogtp")
            for q in range(4):
                d = heads[q]
                p.op("ACT", lambda e, q=q, d=d: e.copy(
                    ofm[:, d, c * 128:(c + 1) * 128], pbx[:, q * 128:(q + 1) * 128]),
                    reads=[kpbx], writes=["ofm"], tag="ogev")
        body.append(s_ogtp)

        def s_state():
            if g == 0:
                p.op("ACT", lambda e, gc=gc: e.activation(
                    bcrow[:],
                    crows[gc % 2][0:1, 0:NH * 128].rearrange("p (h t) -> p h t", h=NH)[:, :, 127:128].squeeze(2),
                    AF.Exp),
                    reads=[f"crows{gc % 2}"], writes=["bcrow"], tag="bc1")
                p.op("PE", lambda e: e.matmul(pfB[X][:, 0:NH], onesrow[:], bcrow[:],
                                              start=True, stop=True),
                     reads=["bcrow", "onesrow"], writes=[kX], tag="bc2", pe_rows=(0, 1))
                p.op("DVE", lambda e: e.tensor_copy(bccol[:], pfB[X][:, 0:NH]),
                     reads=[kX], writes=["bccol"], tag="bc3")
            for q, h in enumerate(heads):
                o = (h % 2) * 64
                p.op("PE", lambda e, q=q, h=h, o=o: e.matmul(
                    pfB[P][o:o + 64, q * 128:(q + 1) * 128], khat[sl][:, q * 64:(q + 1) * 64],
                    sU[sl3][:, q * 128:(q + 1) * 128], start=True, stop=True),
                    reads=[f"khat{sl}", f"sU{sl3}"], writes=[kP], tag="smm")
            for q, h in enumerate(heads):
                o = (h % 2) * 64
                p.op("DVE", lambda e, q=q, h=h, o=o: e.scalar_tensor_tensor(
                    Sb(h), Sm(h), bccol[o:o + 64, h:h + 1],
                    pfB[P][o:o + 64, q * 128:(q + 1) * 128], ALU.mult, ALU.add),
                    reads=["S", "bccol", kP], writes=["Sbf"], tag="subf")
            for q, h in enumerate(heads):
                o = (h % 2) * 64
                p.op("DVE", lambda e, q=q, h=h, o=o: e.scalar_tensor_tensor(
                    Sm(h), Sm(h), bccol[o:o + 64, h:h + 1],
                    pfB[P][o:o + 64, q * 128:(q + 1) * 128], ALU.mult, ALU.add),
                    reads=["S", "bccol", kP], writes=["S"], tag="sup")
            if dh0:
                p.dma("SQ", lambda e: e.dma_start(dbg_sbf0, Sbf[:]), reads=["Sbf"])
                p.dma("SQ", lambda e: e.dma_start(dbg_bcc, bccol[:]), reads=["bccol"])
        body.append(s_state)
        return pro, body


    def emit_rec(t, D=7):
        allsteps = []
        cnt = 0
        for c in range(4):
            for g in range(3):
                if cnt >= rec_limit:
                    break
                cnt += 1
                pro, body = group_steps(t, c, g)
                allsteps.append(pro + body)
        seq = []
        total = D * (len(allsteps) - 1) + max(len(st) for st in allsteps) if allsteps else 0
        for slot in range(total):
            for k in range(len(allsteps)):
                idx = slot - D * k
                if 0 <= idx < len(allsteps[k]):
                    seq.append((t * 12 + k, idx, allsteps[k][idx]))
        for gi_, idx_, st in seq[:rec_steps]:
            p.cur_lid = (gi_, idx_)
            st()
        p.cur_lid = None

    def emit_backend(t):
        p.dma("SQ", lambda e, t=t: e.dma_start(sgb[:], sg_d[t]),
              reads=["sg_done"], writes=["sgb"], tag="sgld")
        p.dma("SQ", lambda e, t=t: e.dma_start(
            xre[:], x_d.rearrange("(s p) d -> p s d", p=128)[:, t * 4:(t + 1) * 4, :]),
            writes=["xre"], tag="xrld")
        for ehalf in range(2):
            for pi in range(4):
                buf = wo_s[pi % 2]
                kb = f"wo{pi % 2}"
                p.dma("GQ", lambda e, pi=pi, buf=buf: e.dma_start(
                    buf[:], wo_d.rearrange("(c p) q -> p c q", p=128)[:, pi * 3:(pi + 1) * 3, :]),
                    writes=[kb], tag="wold")
                for le in range(4):
                    e_ = ehalf * 4 + le
                    for ld in range(3):
                        d = pi * 3 + ld
                        p.op("PE", lambda e, e_=e_, d=d, ld=ld, buf=buf, le=le: e.matmul(
                            pfB[le][:, 0:512], buf[:, ld, e_ * 128:(e_ + 1) * 128], ofm[:, d, :],
                            start=(d == 0), stop=(d == 11)),
                            reads=[kb, "ofm"], writes=[f"pB{le}"], tag="womm")
            for le in range(4):
                e_ = ehalf * 4 + le
                kbank = f"pB{le}"
                krn = f"rno{le % 2}"
                p.op("DVE", lambda e, e_=e_, le=le: e.tensor_mul(
                    rno[le % 2][:], pfB[le][:, 0:512], sgb[:, e_, :]),
                    reads=[kbank, "sgb"], writes=[krn], tag="smul")
                for tc in range(4):
                    p.op("PE", lambda e, le=le, tc=tc: e.transpose(
                        pfB[4 + le % 2][:, tc * 128:(tc + 1) * 128],
                        rno[le % 2][:, tc * 128:(tc + 1) * 128], identf[:]),
                        reads=[krn, "identf"], writes=[f"pB{4 + le % 2}"], tag="ctb")
                for tc in range(4):
                    p.op("DVE", lambda e, le=le, e_=e_, tc=tc: e.tensor_add(
                        xre[:, tc, e_ * 128:(e_ + 1) * 128],
                        pfB[4 + le % 2][:, tc * 128:(tc + 1) * 128],
                        xre[:, tc, e_ * 128:(e_ + 1) * 128]),
                        reads=[f"pB{4 + le % 2}", "xre"], writes=["xre"], tag="comb")
        p.dma("SQ", lambda e, t=t: e.dma_start(
            y_d.rearrange("(s p) d -> p s d", p=128)[:, t * 4:(t + 1) * 4, :], xre[:]),
            reads=["xre"], writes=["xre"], tag="yst")

    if phase == "A":
        p.emit()
        return nc, p
    if dbg:
        dbg_kbb = nc.dram_tensor("dbg_kbb", [128, 2, 256], BF16, kind="Internal").ap()
        dbg_khat = nc.dram_tensor("dbg_khat", [128, 256], BF16, kind="Internal").ap()
        dbg_b0 = nc.dram_tensor("dbg_b0", [128, 512], BF16, kind="Internal").ap()
        dbg_dv = nc.dram_tensor("dbg_dv", [128, 2, 512], BF16, kind="Internal").ap()
        dbg_wkt = nc.dram_tensor("dbg_wkt", [128, 2, 256], BF16, kind="Internal").ap()
        dbg_wkf = nc.dram_tensor("dbg_wkf", [128, 2, 256], BF16, kind="Internal").ap()
        dbg_df = nc.dram_tensor("dbg_df", [128, 512], BF16, kind="Internal").ap()
        dbg_sbf = nc.dram_tensor("dbg_sbf", [128, 6, 128], BF16, kind="Internal").ap()
        dbg_sbf0 = nc.dram_tensor("dbg_sbf0", [128, 6, 128], BF16, kind="Internal").ap()
        dbg_khat0 = nc.dram_tensor("dbg_khat0", [128, 256], BF16, kind="Internal").ap()
        dbg_df0 = nc.dram_tensor("dbg_df0", [128, 512], BF16, kind="Internal").ap()
        dbg_bcc = nc.dram_tensor("dbg_bcc", [128, 12], F32, kind="Internal").ap()
        dbg_q = nc.dram_tensor("dbg_q", [128, 6, TB], BF16, kind="Internal").ap()
        dbg_k = nc.dram_tensor("dbg_k", [128, 6, TB], BF16, kind="Internal").ap()
        dbg_v = nc.dram_tensor("dbg_v", [128, 12, TB], BF16, kind="Internal").ap()
        dbg_g = nc.dram_tensor("dbg_g", [128, 4, VD], BF16, kind="Internal").ap()
        dbg_ofm = nc.dram_tensor("dbg_ofm", [128, 12, TB], BF16, kind="Internal").ap()

    emit_frontend(0)
    if dbg:
        p.dma("SQ", lambda e: e.dma_start(dbg_q, qs_[:]), reads=["qs"])
        p.dma("SQ", lambda e: e.dma_start(dbg_k, ks_[:]), reads=["ks"])
        p.dma("SQ", lambda e: e.dma_start(dbg_v, vs_[:]), reads=["vs"])
        p.dma("SQ", lambda e: e.dma_start(dbg_g, gt_[:]), reads=["gt"])
    if phase == "fe":
        p.emit()
        return nc, p
    for t in range(NTB):
        emit_rec(t)
        if phase == "rec1" and t == 0:
            break
        if t + 1 < NTB:
            emit_frontend(t + 1)
        emit_backend(t)
        if dbg and t == 0:
            p.dma("SQ", lambda e: e.dma_start(dbg_ofm, ofm[:]), reads=["ofm"])

    p.emit()
    return nc, p


_BF = ml_dtypes.bfloat16


def _consts():
    ii = np.arange(128)
    identf = np.eye(128, dtype=np.float32)
    identb = np.eye(128).astype(_BF)
    e64 = ((ii[:, None] // 64) == (ii[None, :] // 64)).astype(_BF)
    utones = (ii[:, None] <= ii[None, :]).astype(np.float32)
    m1 = np.where(ii[:, None] < ii[None, :], 0.0, -1e30).astype(np.float32)
    masks4 = np.ascontiguousarray(np.tile(m1, (1, 4)).astype(_BF))
    i4 = np.ascontiguousarray(np.tile(np.eye(128, dtype=np.float32), (1, 4))).astype(_BF)
    onesrow = np.ones((1, 128), np.float32)
    negonesrow = -np.ones((1, 128), np.float32)
    return dict(identf=identf, identb=identb, e64=e64, utones=utones,
                masks4=masks4, i4=i4, onesrow=onesrow, negonesrow=negonesrow)


def kernel(**inp):
    x = np.ascontiguousarray(np.asarray(inp["L0_attn_hidden_states"], dtype=np.float32))
    B, T, _ = x.shape
    g = lambda k: np.ascontiguousarray(np.asarray(inp[k], dtype=np.float32))
    consts = _consts()
    w1 = g("L0_ffn_W1")
    w2 = g("L0_ffn_W2")
    base = dict(
        wq=g("L0_attn_Wq"), wk=g("L0_attn_Wk"), wv=g("L0_attn_Wv"), wg=g("L0_attn_Wg"),
        wgkb=np.ascontiguousarray(np.concatenate([g("L0_attn_Wgk"), g("L0_attn_Wb")], axis=1)),
        w1t=np.ascontiguousarray(w1.T), w2t=np.ascontiguousarray(w2.T),
        wo=g("L0_attn_Wo"),
        b1=np.ascontiguousarray(g("L0_ffn_b1").reshape(32, 128).T),
        b2=np.ascontiguousarray(g("L0_ffn_b2").reshape(8, 128).T),
        cq=np.ascontiguousarray(g("L0_attn_conv_q").reshape(6, 128, 4).transpose(1, 0, 2)),
        ck=np.ascontiguousarray(g("L0_attn_conv_k").reshape(6, 128, 4).transpose(1, 0, 2)),
        cv=np.ascontiguousarray(g("L0_attn_conv_v").reshape(12, 128, 4).transpose(1, 0, 2)),
        alog=g("L0_attn_A_log").reshape(1, NH),
        dtbb=np.ascontiguousarray(np.concatenate([g("L0_attn_dt_bias"), g("L0_attn_bb")]).reshape(1, 24)),
        gnorm=g("L0_attn_gnorm_w").reshape(1, DV),
        **consts,
    )
    nc, _p = build_program(T)
    in_maps = [dict(base, x=x[b]) for b in range(B)]
    res = run_bass_kernel_spmd(nc, in_maps, core_ids=list(range(B)))
    return np.stack([r["y"] for r in res.results]).astype(np.float32)
